# version 06
# baseline (speedup 1.0000x reference)
import math
import contextlib
import numpy as np
import concourse.bass as bass
import concourse.mybir as mybir
from concourse.bass_utils import run_bass_kernel_spmd

F32 = mybir.dt.float32
BF16 = mybir.dt.bfloat16
U8 = mybir.dt.uint8
AF = mybir.ActivationFunctionType
ALU = mybir.AluOpType
AX = mybir.AxisListType

S = 2048
DM = 4096
CF = 32
LNEPS = 1e-5
RMSEPS = 1e-5


def _k_matrix():
    n = DM
    t = np.arange(n)
    K = np.empty((n, n), dtype=np.float64)
    K[0] = 1.0 / math.sqrt(n)
    f = np.arange(1, (n - 1) // 2 + 1)[:, None]
    ang = 2 * np.pi * f * t[None, :] / n
    K[1:n - 1:2] = np.cos(ang) / math.sqrt(n / 2.0)
    K[2:n:2] = np.sin(ang) / math.sqrt(n / 2.0)
    K[n - 1] = ((-1.0) ** t) / math.sqrt(n)
    hi = K.astype(np.float32)
    lo = (K - hi.astype(np.float64)).astype(np.float32)
    return hi, lo

def _rope_tables():
    D = 128
    try:
        import jax
        import jax.numpy as jnp
        cpu = jax.devices("cpu")[0]
        with jax.default_device(cpu):
            inv_freq = 1.0 / 10000.0 ** (jnp.arange(0, D, 2, dtype=jnp.float32) / D)
            ang = jnp.arange(S, dtype=jnp.float32)[:, None] * inv_freq
            cosj = np.asarray(jnp.cos(ang)).astype(np.float32)
            sinj = np.asarray(jnp.sin(ang)).astype(np.float32)
    except Exception:
        inv = (np.float32(1.0) / (np.float32(10000.0) **
               (np.arange(0, D, 2, dtype=np.float32) / np.float32(D))))
        ang = (np.arange(S, dtype=np.float32)[:, None] * inv).astype(np.float32)
        cosj = np.cos(ang).astype(np.float32)
        sinj = np.sin(ang).astype(np.float32)
    cos = np.concatenate([cosj, cosj], axis=-1)
    sin = np.concatenate([sinj, sinj], axis=-1)
    cosT = np.ascontiguousarray(cos.T).astype(np.float32)
    sinT = np.ascontiguousarray(sin.T).astype(np.float32)
    sinT[0:64] *= -1.0
    return cosT, sinT

def _strips_T(W, nmb):
    fout, fin = W.shape
    cf = fin // 128
    assert fout == nmb * 128
    return np.ascontiguousarray(
        W.reshape(nmb, 128, cf, 128).transpose(0, 3, 2, 1).reshape(nmb, 128, fin))

def _strips_N(W, nmb):
    fin, fout = W.shape
    cf = fin // 128
    assert fout == nmb * 128
    return np.ascontiguousarray(
        W.reshape(cf, 128, nmb, 128).transpose(2, 1, 0, 3).reshape(nmb, 128, fin))

def host_prepare(inputs):
    g = lambda k: np.asarray(inputs[k], dtype=np.float32)
    d = {}
    d["WqkvL"] = _strips_N(g("L0_attn_a_Wqkv"), 24)
    d["bqkv_c"] = np.ascontiguousarray(g("L0_attn_a_bqkv").reshape(24, 128).T)
    d["WoaL"] = _strips_N(g("L0_attn_a_Wo"), 32)
    d["bo_c"] = np.ascontiguousarray(g("L0_attn_a_bo").reshape(32, 128).T)
    d["WbT"] = np.ascontiguousarray(g("L0_ffn_W_base").T)
    d["bb_c"] = np.ascontiguousarray(g("L0_ffn_b_base").reshape(32, 128).T)
    d["enc01"] = np.ascontiguousarray(
        np.concatenate([g("L0_ffn_enc0"), g("L0_ffn_enc1")], axis=0))
    d["KM"], d["KML"] = _k_matrix()
    m0 = np.asarray(inputs["L0_ffn_mask0"]).astype(np.uint8)
    m1 = np.asarray(inputs["L0_ffn_mask1"]).astype(np.uint8)
    lay0 = lambda a: np.ascontiguousarray(a.reshape(3, 32, 128, 32).transpose(0, 2, 1, 3).reshape(3, 128, 1024))
    lay1 = lambda a: np.ascontiguousarray(a.reshape(3, 32, 128, 8).transpose(0, 2, 1, 3).reshape(3, 128, 256))
    d["c0L"] = lay0(g("L0_ffn_coefs0")); d["m0L"] = lay0(m0)
    d["c1L"] = lay1(g("L0_ffn_coefs1")); d["m1L"] = lay1(m1)
    d["WqL"] = _strips_T(g("L0_attn_b_Wq"), 32)
    d["WkL"] = _strips_T(g("L0_attn_b_Wk"), 8)
    d["WvL"] = _strips_T(g("L0_attn_b_Wv"), 8)
    d["WgL"] = _strips_T(g("L0_attn_b_Wg"), 32)
    d["WoL"] = _strips_T(g("L0_attn_b_Wo"), 32)
    d["WinT"] = np.ascontiguousarray(g("L0_attn_b_Win").T)
    d["dtb_r"] = g("L0_attn_b_dt_bias").reshape(1, 32).copy()
    d["alb_r"] = g("L0_attn_b_A_log_bias").reshape(1, 32).copy()
    d["gn_c"] = g("L0_attn_b_gnorm_w").reshape(128, 1).copy()
    cosT, sinT = _rope_tables()
    d["cosT"] = cosT; d["sinTs"] = sinT
    d["IDN"] = np.eye(128, dtype=np.float32)
    d["UTRI"] = np.triu(np.ones((128, 128), np.float32))
    ii = np.arange(128)
    d["BIAST"] = np.where(ii[None, :] >= ii[:, None], 0.0, -1e30).astype(np.float32)
    d["ONE1F"] = np.ones((1, 128), np.float32)
    return d


ENGINES = ("sync", "tensor", "vector", "scalar", "gpsimd")
ENG_SEM = {"tensor": "pe", "vector": "dv", "scalar": "ac", "gpsimd": "po", "sync": "sy"}

def merge(*deplists):
    best = {}
    for dl in deplists:
        for (s, v) in dl:
            if v > best.get(s, 0):
                best[s] = v
    return list(best.items())

class Reg:
    __slots__ = ("w", "r", "multi", "name")
    def __init__(self, name="", multi=False):
        self.w = []
        self.r = []
        self.multi = multi
        self.name = name
    def turnover(self):
        base = merge(self.w, self.r)
        self.w = []
        self.r = base
        return base

class Walk:
    def __init__(self, nc, es):
        self.nc = nc
        self.es = es
        self.steps = []
        self.sems = {}
        self.cnt = {}

    def sem(self, name):
        if name not in self.sems:
            self.sems[name] = self.es.enter_context(self.nc.semaphore(name))
            self.cnt[name] = 0
        return self.sems[name]

    def op(self, eng, fn, wait=(), inc=()):
        wait = [(s, v) for (s, v) in wait if v > 0]
        for s, n in inc:
            self.sem(s)
            self.cnt[s] += n
        for s, _ in wait:
            self.sem(s)
        self.steps.append((eng, fn, tuple(wait), tuple(inc)))

    def now(self, s):
        self.sem(s)
        return self.cnt[s]

    def emit(self):
        with self.nc.Block() as block:
            for engname in ENGINES:
                mysteps = [st for st in self.steps if st[0] == engname]
                if not mysteps:
                    continue
                def body(e, mysteps=mysteps):
                    lastw = {}
                    for (_, fn, waits, incs) in mysteps:
                        for (s, v) in waits:
                            if v > lastw.get(s, 0):
                                e.wait_ge(self.sems[s], v)
                                lastw[s] = v
                        inst = fn(e)
                        for (s, n) in incs:
                            inst.then_inc(self.sems[s], n)
                getattr(block, engname)(body)


class B:
    def __init__(self, stages=4):
        self.es = contextlib.ExitStack()
        self.stages = stages
        self.nc = bass.Bass("TRN2", target_bir_lowering=False)
        self.w = Walk(self.nc, self.es)
        self.dbg_outs = {}
        self._declare_dram()
        self._alloc_sbuf()
        self._alloc_psum()
        self._regions()

    def E(self, eng, fn, reads=(), writes=(), extra=(), stream=None):
        wl = [list(extra)]
        for r in reads:
            wl.append(r.w)
        for wr in writes:
            wl.append(wr.r)
            if not wr.multi:
                wl.append(wr.w)
        waits = merge(*wl)
        if eng == "tensor":
            waits = [(s, v) for (s, v) in waits if s != "pe"]
        sem = stream or ENG_SEM[eng]
        amt = 16 if stream else 1
        self.w.op(eng, fn, wait=waits, inc=[(sem, amt)])
        dep = [(sem, self.w.now(sem))]
        for r in reads:
            r.r = merge(r.r, dep)
        for wr in writes:
            if wr.multi:
                wr.w = merge(wr.w, dep)
            else:
                wr.w = dep
                wr.r = []
        return dep

    def DMA(self, eng, out, in_, stream, reads=(), writes=(), extra=()):
        return self.E(eng, lambda e, o=out, i=in_: e.dma_start(out=o, in_=i),
                      reads=reads, writes=writes, extra=extra, stream=stream)

    def _declare_dram(self):
        nc = self.nc
        I = lambda n, s, dt=F32: nc.dram_tensor(n, s, dt, kind="ExternalInput").ap()
        T = lambda n, s, dt=F32: nc.dram_tensor(n, s, dt, kind="Internal").ap()
        self.xT0 = I("xT0", [DM, S])
        self.WqkvL = I("WqkvL", [24, 128, DM]); self.bqkv_c = I("bqkv_c", [128, 24])
        self.WoaL = I("WoaL", [32, 128, 1024]); self.bo_c = I("bo_c", [128, 32])
        self.WbT_d = I("WbT", [DM, DM]); self.bb_c = I("bb_c", [128, 32])
        self.enc01_d = I("enc01", [40, DM]); self.KM = I("KM", [DM, DM])
        self.KML = I("KML", [DM, DM])
        self.c0L = I("c0L", [3, 128, 1024]); self.m0L = I("m0L", [3, 128, 1024], U8)
        self.c1L = I("c1L", [3, 128, 256]); self.m1L = I("m1L", [3, 128, 256], U8)
        self.WqL = I("WqL", [32, 128, DM]); self.WkL = I("WkL", [8, 128, DM])
        self.WvL = I("WvL", [8, 128, DM]); self.WgL = I("WgL", [32, 128, DM])
        self.WoL = I("WoL", [32, 128, DM]); self.WinT_d = I("WinT", [DM, 32])
        self.dtb_r = I("dtb_r", [1, 32]); self.alb_r = I("alb_r", [1, 32])
        self.gn_c = I("gn_c", [128, 1])
        self.cosT_d = I("cosT", [128, S]); self.sinTs_d = I("sinTs", [128, S])
        self.IDN_d = I("IDN", [128, 128]); self.UTRI_d = I("UTRI", [128, 128])
        self.BIAST_d = I("BIAST", [128, 128]); self.ONE1F_d = I("ONE1F", [1, 128])
        self.xT1 = T("xT1", [DM, S]); self.xT2 = T("xT2", [DM, S]); self.xT3 = T("xT3", [DM, S])
        self.WqkvB = T("WqkvB", [24, 128, DM], BF16)
        self.WoaB = T("WoaB", [32, 128, 1024], BF16)
        self.WeffB = T("WeffB", [32, 128, DM], BF16)
        self.WqB = T("WqB", [32, 128, DM], BF16); self.WkB = T("WkB", [8, 128, DM], BF16)
        self.WvB = T("WvB", [8, 128, DM], BF16); self.WgB = T("WgB", [8 * 4, 128, DM], BF16)
        self.WoB = T("WoB", [32, 128, DM], BF16)
        self.qTa = T("qTa", [8, 128, S], BF16); self.kTa = T("kTa", [8, 128, S], BF16)
        self.vta = T("vta", [16, 128, 1024], BF16); self.avTa = T("avTa", [8, 128, S], BF16)
        self.yTb = T("yTb", [32, 128, S], BF16)
        self.yT = nc.dram_tensor("yT", [DM, S], F32, kind="ExternalOutput").ap()

    def dbg(self, name, shape, dt=F32):
        ap = self.nc.dram_tensor(name, shape, dt, kind="ExternalOutput").ap()
        self.dbg_outs[name] = ap
        return ap

    def _alloc_sbuf(self):
        nc = self.nc
        RES_BYTES = 204800
        resv = nc.alloc_sbuf_tensor("RESV", [128, RES_BYTES], U8)
        base0 = nc.lookup_mloc(resv).addr
        off = [base0]
        def at(name, shape, dt, o=None):
            if o is None:
                o = off[0]
            itm = 2 if dt == BF16 else (1 if dt == U8 else 4)
            nbytes = int(np.prod(shape[1:])) * itm
            h = nc.alloc_sbuf_tensor_at(name, shape, dt, offset=o)
            off[0] = max(off[0], o + ((nbytes + 31) // 32) * 32)
            return h
        self.BIG = at("BIG", [128, 32768], BF16)
        self.WS = at("WS", [128, 8192], BF16)
        self.MUB = at("MUB", [128, S], F32)
        self.RSB = at("RSB", [128, S], F32)
        self.XIN = at("XIN", [128, 1024], F32)
        self.T1B = at("T1B", [128, 1024], F32)
        self.RES = at("RES", [128, 1024], F32)
        self.XNEW = at("XNEW", [128, 1024], F32)
        self.EVB = at("EVB", [128, 2048], BF16)
        self.SROW = at("SROW", [1, 2048], F32)
        self.EXPAT = at("EXPAT", [128, 2048], BF16)
        self.ATTNT = at("ATTNT", [128, 2048], BF16)
        self.IDN = at("IDNs", [128, 128], F32)
        self.IDNB = at("IDNBs", [128, 128], BF16)
        self.UTRI = at("UTRIs", [128, 128], F32)
        self.BIAST = at("BIASTs", [128, 128], F32)
        self.WINTB = at("WINTB", [128, 1024], BF16)
        self.ONE1F = at("ONE1Fs", [1, 128], F32)
        self.ONECB = at("ONECB", [128, 1], BF16)
        self.BQKV = at("BQKVs", [128, 24], F32)
        self.BOA = at("BOAs", [128, 32], F32)
        self.BBC = at("BBCs", [128, 32], F32)
        self.GN = at("GNs", [128, 1], F32)
        self.DTBB = at("DTBB", [128, 32], F32)
        self.ABB = at("ABB", [128, 32], F32)
        self.DTBR = at("DTBR", [1, 32], F32)
        self.ALBR = at("ALBR", [1, 32], F32)
        self.AROW = at("AROW", [1, 32], F32)
        self.TRS = at("TRS", [128, 768], BF16)
        self.MCOL = at("MCOL", [128, 16], F32)
        self.EPSB = at("EPSB", [128, 1], F32)
        self.DTQ = at("DTQ", [128, 192], F32)
        self.GGQ = at("GGQ", [128, 192], F32)
        self.GQ = at("GQ", [128, 192], F32)
        self.EGQ = at("EGQ", [128, 192], F32)
        self.EDQ = at("EDQ", [128, 192], F32)
        self.CDQ = at("CDQ", [128, 192], F32)
        self.GTB = at("GTB", [32, 768], F32)
        self.GROWH = at("GROWH", [1, 768], F32)
        self.SSH = at("SSH", [128, 32], F32)
        self.RSTDH = at("RSTDH", [128, 32], F32)
        self.RSROW = at("RSROW", [1, 768], F32)
        self.DECT = at("DECT", [128, 128], F32)
        self.YSB = at("YSB", [128, 128], F32)
        self.STB = at("STB", [128, 128], BF16)
        self.MU8 = at("MU8", [128, 1024], U8)
        self.RP2 = at("RP2", [128, 768], BF16)
        self.SDT = at("SDT", [128, 128], BF16)
        self.KDT = at("KDT", [128, 128], BF16)
        self.XCT = at("XCT", [128, 128], BF16)
        self.YB16 = at("YB16", [128, 128], BF16)
        self.RSTT = at("RSTT", [8, 128], F32)
        self.GLB = at("GLB", [128, 32], F32)
        self.TRS2 = at("TRS2", [128, 128], BF16)
        self.W01SB = at("W01SB", [128, 1280], BF16)
        self.MACC = at("MACC", [40, 1024], F32)
        self.STG32 = at("STG32", [128, 2048], F32)
        self.STGBF = at("STGBF", [128, 2048], BF16)
        base = off[0]
        self.STATES = at("STATES", [128, 4096], F32, o=base)
        self.ROPEC = at("ROPEC", [128, 768], F32, o=base + 16384)
        self.ROPES = at("ROPES", [128, 768], F32, o=base + 16384 + 3072)
        end_b = base + 16384 + 6144
        self.W01S = at("W01S", [128, 1280], F32, o=base)
        self.ENC01B = at("ENC01B", [40, DM], BF16, o=base + 5120)
        self.MBF = at("MBF", [40, DM], BF16, o=base + 13312)
        off[0] = max(end_b, base + 13312 + 8192)
        self.sbuf_used = off[0]
        assert off[0] <= base0 + RES_BYTES, f"arena overflow: {off[0]}"

    def _alloc_psum(self):
        nc = self.nc
        self.PKS = self.es.enter_context(nc.psum_tensor("PKS", [128, 2048], F32))
        self.PK4H = self.es.enter_context(nc.psum_tensor("PK4", [128, 512], F32))
        self.PK5H = self.es.enter_context(nc.psum_tensor("PK5", [128, 512], F32))
        self.PTA = self.es.enter_context(nc.psum_tensor("PTA", [128, 1024], BF16))
        self.PTB = self.es.enter_context(nc.psum_tensor("PTB", [128, 1024], BF16))

    def pk(self, i, rows=None):
        if i < 4:
            return self.PKS[:, i * 512:(i + 1) * 512] if rows is None \
                else self.PKS[rows[0]:rows[1], i * 512:(i + 1) * 512]
        h = self.PK4H if i == 4 else self.PK5H
        return h[:, :] if rows is None else h[rows[0]:rows[1], :]

    def _regions(self):
        R = lambda n, m=False: Reg(n, m)
        self.rPK = [R(f"pk{i}") for i in range(6)]
        self.rPT = [R("pta"), R("ptb")]
        self.rXIN = [R("xin0"), R("xin1")]
        self.rT1 = R("t1")
        self.rWS = [R("ws0"), R("ws1")]
        self.rEVB = [R(f"evb{i}") for i in range(4)]
        self.rRES = [R("res0"), R("res1")]
        self.rXNEW = [R("xn0"), R("xn1")]
        self.rSTG = [R("stg0"), R("stg1")]
        self.rSTGB = [R("stgb0"), R("stgb1")]
        self.rMUB = R("mub", True)
        self.rRSB = R("rsb", True)
        self.rSROW = [R(f"srow{i}") for i in range(4)]
        self.rMCOL = [R(f"mcol{i}") for i in range(16)]
        self.rEXP = [R(f"exp{i}") for i in range(4)]
        self.rATT = [R(f"att{i}") for i in range(16)]
        self.rTRS = [R("trs0"), R("trs1"), R("trs2")]
        self.rCONST = R("const", True)
        self.rBIGA = R("bigA", True)
        self.rBIGB = R("bigB", True)
        self.rBIGC = R("bigC", True)
        self.big_live = [self.rBIGA, self.rBIGB, self.rBIGC]
        self.rXT = {0: R("xt0"), 1: R("xt1", True), 2: R("xt2", True), 3: R("xt3", True),
                    4: R("yt", True)}
        self.rXTg = {k: [R(f"xt{k}g{i}", True) for i in range(4)] for i2 in [0] for k in (1, 2, 3, 4)}
        self.rWB = {k: R(f"wb_{k}", True) for k in
                    ("Wqkv", "Woa", "Weff", "Wq", "Wk", "Wv", "Wg", "Wo")}
        self.rQKVa = R("qkva", True)
        self.rAVTa = R("avta", True)
        self.rYTb = R("ytb", True)

    def ph_consts(self):
        d = self
        for dst, src in [(d.IDN, d.IDN_d), (d.UTRI, d.UTRI_d), (d.BIAST, d.BIAST_d),
                         (d.ONE1F, d.ONE1F_d), (d.BQKV, d.bqkv_c), (d.BOA, d.bo_c),
                         (d.BBC, d.bb_c), (d.GN, d.gn_c), (d.DTBR, d.dtb_r),
                         (d.ALBR, d.alb_r)]:
            self.DMA("sync", dst[:], src, "cl", writes=[d.rCONST])
        rAR = Reg("arow")
        self.E("vector", lambda v: v.tensor_copy(d.IDNB[:], d.IDN[:]),
               reads=[d.rCONST], writes=[d.rCONST])
        self.E("vector", lambda v: v.memset(d.ONECB[:], 1.0), writes=[d.rCONST])
        self.E("vector", lambda v: v.memset(d.EPSB[:], LNEPS), writes=[d.rCONST])
        self.E("scalar", lambda s: s.activation(d.AROW[:], d.ALBR[:], AF.Exp),
               reads=[d.rCONST], writes=[rAR])
        self.E("vector", lambda v: v.tensor_scalar(d.AROW[:], d.AROW[:], -1.0, None, ALU.mult),
               reads=[rAR], writes=[rAR])
        self.E("tensor", lambda t: t.matmul(self.pk(5)[:, 0:32], d.ONE1F[:], d.DTBR[:],
                                            start=True, stop=True),
               reads=[d.rCONST], writes=[self.rPK[5]])
        self.E("tensor", lambda t: t.matmul(self.pk(5)[:, 32:64], d.ONE1F[:], d.AROW[:],
                                            start=True, stop=True),
               reads=[d.rCONST, rAR], writes=[self.rPK[5]])
        self.E("vector", lambda v: v.tensor_copy(d.DTBB[:], self.pk(5)[:, 0:32]),
               reads=[self.rPK[5]], writes=[d.rCONST])
        self.E("vector", lambda v: v.tensor_copy(d.ABB[:], self.pk(5)[:, 32:64]),
               reads=[self.rPK[5]], writes=[d.rCONST])

    def ph_cast(self, srcL, dstB, nmb, fin, jobid, wreg):
        d = self
        nh = fin // 1024
        if not hasattr(self, "_cast_ctr"):
            self._cast_ctr = 0
        for mb in range(nmb):
            for h in range(nh):
                buf = self._cast_ctr % 2
                self._cast_ctr += 1
                sl32 = d.STG32[:, buf * 1024:(buf + 1) * 1024]
                slbf = d.STGBF[:, buf * 1024:(buf + 1) * 1024]
                self.DMA("gpsimd", sl32, srcL[mb, :, h * 1024:(h + 1) * 1024],
                         f"c{jobid}l{buf}", writes=[d.rSTG[buf]])
                self.E("gpsimd", lambda g, a=slbf, b=sl32: g.tensor_copy(a, b),
                       reads=[d.rSTG[buf]], writes=[d.rSTGB[buf]])
                self.DMA("gpsimd", dstB[mb, :, h * 1024:(h + 1) * 1024], slbf,
                         f"c{jobid}s{buf}", reads=[d.rSTGB[buf]], writes=[wreg])
                yield

    def ph_stats(self, xk, xsrc):
        d = self
        self.rMUB.turnover()
        self.rRSB.turnover()
        for tg in range(4):
            for c in range(CF):
                b = c % 2
                xs = d.XIN[:, b * 512:(b + 1) * 512]
                xb = d.EVB[:, b * 512:(b + 1) * 512]
                sq = d.EVB[:, 1024 + b * 512:1024 + (b + 1) * 512]
                self.DMA("sync", xs, xsrc[c * 128:(c + 1) * 128, tg * 512:(tg + 1) * 512],
                         f"xi{b}", reads=[self.xreg(xk, tg)], writes=[d.rXIN[b]])
                self.E("vector", lambda v, a=xb, s_=xs: v.tensor_copy(a, s_),
                       reads=[d.rXIN[b]], writes=[d.rEVB[b]])
                self.E("scalar", lambda s, a=sq, b_=xb: s.activation(a, b_, AF.Square),
                       reads=[d.rEVB[b]], writes=[d.rEVB[2 + b]])
                self.E("tensor", lambda t, a=xb, st=(c == 0), z=(c == CF - 1): t.matmul(
                    self.pk(4, (0, 1)), d.ONECB[:], a, start=st, stop=z),
                    reads=[d.rCONST, d.rEVB[b]],
                    writes=[self.rPK[4]])
                self.E("tensor", lambda t, a=sq, st=(c == 0), z=(c == CF - 1): t.matmul(
                    self.pk(5, (0, 1)), d.ONECB[:], a, start=st, stop=z),
                    reads=[d.rCONST, d.rEVB[2 + b]],
                    writes=[self.rPK[5]])
                yield
            self._stats_rows(tg * 512, 512)

    def _stats_rows(self, tok0, wdt, sm=4, sq=5):
        d = self
        mu_r = d.SROW[0:1, 0:wdt]; sq_r = d.SROW[0:1, 512:512 + wdt]
        var_r = d.SROW[0:1, 1024:1024 + wdt]
        rM, rQ, rV = d.rSROW[0], d.rSROW[1], d.rSROW[2]
        pks = lambda i, r=None: self.pk(i, r)
        self.E("vector", lambda v: v.tensor_scalar(mu_r, self.pk(sm, (0, 1))[:, 0:wdt],
                                                   1.0 / DM, None, ALU.mult),
               reads=[self.rPK[sm]], writes=[rM])
        self.E("vector", lambda v: v.tensor_scalar(sq_r, self.pk(sq, (0, 1))[:, 0:wdt],
                                                   1.0 / DM, None, ALU.mult),
               reads=[self.rPK[sq]], writes=[rQ])
        self.E("vector", lambda v: v.tensor_mul(var_r, mu_r, mu_r), reads=[rM], writes=[rV])
        self.E("vector", lambda v: v.tensor_sub(var_r, sq_r, var_r), reads=[rQ, rV], writes=[rV])
        self.E("scalar", lambda s: s.activation(var_r, var_r, AF.Ln, bias=d.EPSB[0:1, :]),
               reads=[rV, d.rCONST], writes=[rV])
        self.E("scalar", lambda s: s.activation(var_r, var_r, AF.Exp, scale=-0.5),
               reads=[rV], writes=[rV])
        self.E("tensor", lambda t: t.matmul(self.pk(sm)[:, 0:wdt], d.ONE1F[:], mu_r,
                                            start=True, stop=True),
               reads=[d.rCONST, rM], writes=[self.rPK[sm]])
        self.E("tensor", lambda t: t.matmul(self.pk(sq)[:, 0:wdt], d.ONE1F[:], var_r,
                                            start=True, stop=True),
               reads=[d.rCONST, rV], writes=[self.rPK[sq]])
        self.E("vector", lambda v: v.tensor_copy(d.MUB[:, tok0:tok0 + wdt],
                                                 self.pk(sm)[:, 0:wdt]),
               reads=[self.rPK[sm]], writes=[d.rMUB])
        self.E("vector", lambda v: v.tensor_copy(d.RSB[:, tok0:tok0 + wdt],
                                                 self.pk(sq)[:, 0:wdt]),
               reads=[self.rPK[sq]], writes=[d.rRSB])

    def xreg(self, k, g):
        return self.rXT[0] if k == 0 else self.rXTg[k][g]

    def bigseed(self, *regs):
        base = merge(*[r.w for r in regs], *[r.r for r in regs])
        for r in regs:
            r.w = []
            r.r = base

    def bignew(self, n):
        base = merge(*[r.w for r in self.big_live], *[r.r for r in self.big_live])
        out = []
        for i in range(n):
            r = Reg(f"big{i}", multi=True)
            r.r = list(base)
            out.append(r)
        self.big_live = out
        return out

    def ph_xm(self, xk, xsrc, t0, tw, big_off, bigreg):
        d = self
        assert tw <= 1024
        ctr = 0
        for c in range(CF):
            for s0 in range(0, tw, 512):
                w = min(512, tw - s0)
                b = ctr % 2
                ctr += 1
                tt = t0 + s0
                xreads = [self.xreg(xk, g) for g in range(tt // 512, (tt + w - 1) // 512 + 1)]
                xs = d.XIN[:, b * 512:b * 512 + w]
                dst = d.BIG[:, big_off + c * tw + s0: big_off + c * tw + s0 + w]
                self.DMA("sync", xs, xsrc[c * 128:(c + 1) * 128, tt:tt + w],
                         f"xi{b}", reads=xreads, writes=[d.rXIN[b]])
                self.E("vector", lambda v, s_=xs, w=w, tt=tt: v.tensor_tensor(
                    d.T1B[:, 0:w], s_, d.MUB[:, tt:tt + w], ALU.subtract),
                    reads=[d.rXIN[b], d.rMUB], writes=[d.rT1])
                self.E("vector", lambda v, o=dst, w=w, tt=tt: v.tensor_tensor(
                    o, d.T1B[:, 0:w], d.RSB[:, tt:tt + w], ALU.mult),
                    reads=[d.rT1, d.rRSB], writes=[bigreg])
            yield

    def ph_attn_a(self):
        d = self
        yield from self.ph_stats(0, self.xT0)
        for h in range(2):
            self.rBIGA.turnover()
            yield from self.ph_xm(0, self.xT0, h * 1024, 1024, 0, self.rBIGA)
            for mb in range(24):
                wb = mb % 2
                ws = d.WS[:, wb * 4096:(wb + 1) * 4096]
                self.DMA("sync", ws, self.WqkvB[mb], f"ws{wb}",
                         reads=[self.rWB["Wqkv"]], writes=[d.rWS[wb]])
                pki = [(mb % 2) * 2, (mb % 2) * 2 + 1]
                def emit(t, ws=ws, i0=pki[0], i1=pki[1]):
                    last = None
                    for c in range(CF):
                        t.matmul(self.pk(i0), ws[:, c * 128:(c + 1) * 128],
                                 d.BIG[:, c * 1024: c * 1024 + 512],
                                 start=(c == 0), stop=(c == CF - 1))
                        last = t.matmul(self.pk(i1), ws[:, c * 128:(c + 1) * 128],
                                        d.BIG[:, c * 1024 + 512: c * 1024 + 1024],
                                        start=(c == 0), stop=(c == CF - 1))
                    return last
                self.E("tensor", emit, reads=[d.rWS[wb], self.rBIGA],
                       writes=[self.rPK[pki[0]], self.rPK[pki[1]]])
                for ng in range(2):
                    eb = mb % 2
                    ev = d.EVB[:, eb * 1024 + ng * 512: eb * 1024 + (ng + 1) * 512]
                    evr = d.rEVB[eb * 2 + ng]
                    self.E("vector", lambda v, o=ev, p=self.pk(pki[ng]), m=mb: v.tensor_scalar(
                        o, p, d.BQKV[:, m:m + 1], None, ALU.add),
                        reads=[self.rPK[pki[ng]], d.rCONST], writes=[evr])
                    if mb < 16:
                        dst = (self.qTa[mb] if mb < 8 else self.kTa[mb - 8])
                        self.DMA("scalar", dst[:, h * 1024 + ng * 512: h * 1024 + (ng + 1) * 512],
                                 ev, f"ev{eb * 2 + ng}", reads=[evr], writes=[self.rQKVa])
                    else:
                        for tt in range(4):
                            pb = tt % 2
                            pt = self.PTA if pb == 0 else self.PTB
                            self.E("tensor", lambda t, p=pt, i=ev, tt=tt: t.transpose(
                                p[:, 0:128], i[:, tt * 128:(tt + 1) * 128], d.IDNB[:]),
                                reads=[evr, d.rCONST], writes=[self.rPT[pb]])
                            ev2 = d.TRS[:, pb * 128:(pb + 1) * 128]
                            self.E("vector", lambda v, o=ev2, p=pt: v.tensor_copy(o, p[:, 0:128]),
                                   reads=[self.rPT[pb]], writes=[d.rTRS[pb]])
                            tc = h * 8 + ng * 4 + tt
                            self.DMA("scalar", self.vta[tc, :, (mb - 16) * 128:(mb - 15) * 128],
                                     ev2, f"vt{pb}", reads=[d.rTRS[pb]], writes=[self.rQKVa])
                yield
        self.bigseed(self.rBIGA, self.rBIGB)
        for dc in range(8):
            self.DMA("sync", d.BIG[:, dc * 2048:(dc + 1) * 2048], self.kTa[dc], "akv",
                     reads=[self.rQKVa], writes=[self.rBIGA])
        for tc in range(16):
            self.DMA("sync", d.BIG[:, 16384 + tc * 1024:16384 + (tc + 1) * 1024],
                     self.vta[tc], "akv", reads=[self.rQKVa], writes=[self.rBIGB])
        tot = [("akv", self.w.now("akv"))]
        self.rBIGA.w = merge(self.rBIGA.w, tot)
        self.rBIGB.w = merge(self.rBIGB.w, tot)
        for qt in range(16):
            qb = qt % 2
            qtl = d.EVB[:, qb * 1024:(qb + 1) * 1024]
            qreg = [d.rEVB[qb * 2], d.rEVB[qb * 2 + 1]]
            for dc in range(8):
                self.DMA("sync", qtl[:, dc * 128:(dc + 1) * 128],
                         self.qTa[dc, :, qt * 128:(qt + 1) * 128], f"qt{qb}{dc % 2}",
                         reads=[self.rQKVa], writes=[qreg[dc % 2]])
            for ng in range(4):
                def emit(t, ng=ng, qtl=qtl):
                    last = None
                    for dc in range(8):
                        last = t.matmul(self.pk(ng), qtl[:, dc * 128:(dc + 1) * 128],
                                        d.BIG[:, dc * 2048 + ng * 512: dc * 2048 + (ng + 1) * 512],
                                        start=(dc == 0), stop=(dc == 7))
                    return last
                self.E("tensor", emit, reads=qreg + [self.rBIGA], writes=[self.rPK[ng]])
            self.E("vector", lambda v: v.reduce_max(d.MCOL[:, 4:5], self.PKS[:, :], axis=AX.X),
                   reads=[self.rPK[0], self.rPK[1], self.rPK[2], self.rPK[3]],
                   writes=[d.rMCOL[4]])
            self.E("scalar", lambda s: s.activation(d.MCOL[:, 6:7], d.MCOL[:, 4:5], AF.Copy,
                                                    scale=-1.0 / 32.0),
                   reads=[d.rMCOL[4]], writes=[d.rMCOL[6]])
            for ng in range(4):
                self.E("scalar", lambda s, ng=ng: s.activation(
                    d.EXPAT[:, ng * 512:(ng + 1) * 512], self.pk(ng), AF.Exp,
                    bias=d.MCOL[:, 6:7], scale=1.0 / 32.0,
                    accum_out=d.MCOL[:, 8 + ng:9 + ng]),
                    reads=[self.rPK[ng], d.rMCOL[6]],
                    writes=[d.rEXP[ng], d.rMCOL[8 + ng]])
            self.E("gpsimd", lambda g: g.tensor_add(d.MCOL[:, 12:13], d.MCOL[:, 8:9], d.MCOL[:, 9:10]),
                   reads=[d.rMCOL[8], d.rMCOL[9]], writes=[d.rMCOL[12]])
            self.E("gpsimd", lambda g: g.tensor_add(d.MCOL[:, 13:14], d.MCOL[:, 10:11], d.MCOL[:, 11:12]),
                   reads=[d.rMCOL[10], d.rMCOL[11]], writes=[d.rMCOL[13]])
            self.E("gpsimd", lambda g: g.tensor_add(d.MCOL[:, 12:13], d.MCOL[:, 12:13], d.MCOL[:, 13:14]),
                   reads=[d.rMCOL[13]], writes=[d.rMCOL[12]])
            self.E("scalar", lambda s: s.activation(d.MCOL[:, 14:15], d.MCOL[:, 12:13], AF.Ln),
                   reads=[d.rMCOL[12]], writes=[d.rMCOL[14]])
            self.E("scalar", lambda s: s.activation(d.MCOL[:, 14:15], d.MCOL[:, 14:15], AF.Exp,
                                                    scale=-1.0),
                   reads=[d.rMCOL[14]], writes=[d.rMCOL[14]])
            for ck in range(16):
                pb = ck % 2
                pt = self.PTA if pb == 0 else self.PTB
                self.E("tensor", lambda t, p=pt, ck=ck: t.transpose(
                    p[:, 0:128], d.EXPAT[:, ck * 128:(ck + 1) * 128], d.IDNB[:]),
                    reads=[d.rEXP[ck // 4], d.rCONST], writes=[self.rPT[pb]])
                self.E("vector", lambda v, p=pt, ck=ck: v.tensor_copy(
                    d.ATTNT[:, ck * 128:(ck + 1) * 128], p[:, 0:128]),
                    reads=[self.rPT[pb]], writes=[d.rATT[ck]])
            for ngd in range(2):
                def emit(t, ngd=ngd):
                    last = None
                    for ck in range(16):
                        last = t.matmul(self.pk(4 + ngd), d.ATTNT[:, ck * 128:(ck + 1) * 128],
                                        d.BIG[:, 16384 + ck * 1024 + ngd * 512:
                                              16384 + ck * 1024 + (ngd + 1) * 512],
                                        start=(ck == 0), stop=(ck == 15))
                    return last
                self.E("tensor", emit, reads=d.rATT + [self.rBIGB], writes=[self.rPK[4 + ngd]])
            for ngd in range(2):
                self.E("vector", lambda v, ngd=ngd: v.tensor_scalar(
                    d.ATTNT[:, ngd * 512:(ngd + 1) * 512], self.pk(4 + ngd),
                    d.MCOL[:, 14:15], None, ALU.mult),
                    reads=[self.rPK[4 + ngd], d.rMCOL[14]],
                    writes=[d.rATT[ngd * 4 + k] for k in range(4)])
            for dc in range(8):
                pb = dc % 2
                pt = self.PTA if pb == 0 else self.PTB
                self.E("tensor", lambda t, p=pt, dc=dc: t.transpose(
                    p[:, 0:128], d.ATTNT[:, dc * 128:(dc + 1) * 128], d.IDNB[:]),
                    reads=[d.rATT[dc], d.rCONST], writes=[self.rPT[pb]])
                self.E("vector", lambda v, p=pt, dc=dc: v.tensor_copy(
                    d.ATTNT[:, 1024 + dc * 128:1024 + (dc + 1) * 128], p[:, 0:128]),
                    reads=[self.rPT[pb]], writes=[d.rATT[8 + dc]])
                self.DMA("scalar", self.avTa[dc, :, qt * 128:(qt + 1) * 128],
                         d.ATTNT[:, 1024 + dc * 128:1024 + (dc + 1) * 128], f"avs{dc}",
                         reads=[d.rATT[8 + dc]], writes=[self.rAVTa])
            yield

    def ph_attn_a_out(self):
        d = self
        self.rMUB.turnover()
        self.rRSB.turnover()
        self.bigseed(self.rBIGC, self.rBIGA, self.rBIGB)
        for og in range(4):
            self.rBIGC.turnover()
            for dc in range(8):
                self.DMA("sync", d.BIG[:, dc * 512:(dc + 1) * 512],
                         self.avTa[dc, :, og * 512:(og + 1) * 512], "sl",
                         reads=[self.rAVTa], writes=[self.rBIGC])
            for mb in range(32):
                wb = mb % 2
                ws = d.WS[:, wb * 4096: wb * 4096 + 1024]
                self.DMA("sync", ws, self.WoaB[mb], f"ws{wb}",
                         reads=[self.rWB["Woa"]], writes=[d.rWS[wb]])
                pki = mb % 2
                rb = mb % 2
                res = d.RES[:, rb * 512:(rb + 1) * 512]
                self.DMA("sync", res, self.xT0[mb * 128:(mb + 1) * 128, og * 512:(og + 1) * 512],
                         f"xr{rb}", reads=[self.rXT[0]], writes=[d.rRES[rb]])
                def emit(t, ws=ws, i=pki):
                    last = None
                    for dc in range(8):
                        last = t.matmul(self.pk(i), ws[:, dc * 128:(dc + 1) * 128],
                                        d.BIG[:, dc * 512:(dc + 1) * 512],
                                        start=(dc == 0), stop=(dc == 7))
                    return last
                self.E("tensor", emit, reads=[d.rWS[wb], self.rBIGC], writes=[self.rPK[pki]])
                xb = mb % 2
                xn = d.XNEW[:, xb * 512:(xb + 1) * 512]
                self.E("vector", lambda v, o=xn, p=self.pk(pki), m=mb, r=res: v.scalar_tensor_tensor(
                    o, p, d.BOA[:, m:m + 1], r, ALU.add, ALU.add),
                    reads=[self.rPK[pki], d.rCONST, d.rRES[rb]], writes=[d.rXNEW[xb]])
                self.DMA("scalar", self.xT1[mb * 128:(mb + 1) * 128, og * 512:(og + 1) * 512],
                         xn, f"xw{xb}", reads=[d.rXNEW[xb]], writes=[self.rXTg[1][og]])
                eb = mb % 2
                xbf = d.EVB[:, eb * 512:(eb + 1) * 512]
                sqb = d.EVB[:, 1024 + eb * 512:1024 + (eb + 1) * 512]
                self.E("vector", lambda v, a=xbf, i=xn: v.tensor_copy(a, i),
                       reads=[d.rXNEW[xb]], writes=[d.rEVB[eb]])
                self.E("scalar", lambda s, a=sqb, i=xbf: s.activation(a, i, AF.Square),
                       reads=[d.rEVB[eb]], writes=[d.rEVB[2 + eb]])
                self.E("tensor", lambda t, a=xbf, st=(mb == 0), z=(mb == 31): t.matmul(
                    self.pk(4, (0, 1)), d.ONECB[:], a, start=st, stop=z),
                    reads=[d.rCONST, d.rEVB[eb]], writes=[self.rPK[4]])
                self.E("tensor", lambda t, a=sqb, st=(mb == 0), z=(mb == 31): t.matmul(
                    self.pk(5, (0, 1)), d.ONECB[:], a, start=st, stop=z),
                    reads=[d.rCONST, d.rEVB[2 + eb]], writes=[self.rPK[5]])
                yield
            self._stats_rows(og * 512, 512)

    def finish(self):
        allw = [(s, self.w.cnt[s]) for s in list(self.w.cnt)]
        self.w.op("sync", lambda e: e.nop(), wait=merge(allw))

    def ph_prep(self):
        d = self
        rW01 = Reg("w01s")
        rCst = Reg("cstg"); rM8 = Reg("m8"); rMskf = Reg("mskf")
        w01v = d.W01S[:, 0:1280].rearrange("p (c r) -> p c r", r=40)
        for part, (cl, ml, R, roff) in enumerate(
                [(d.c0L, d.m0L, 32, 0), (d.c1L, d.m1L, 8, 32)]):
            for e in range(3):
                n = 32 * R
                self.DMA("sync", d.RES[:, 0:n], cl[e], "cof", writes=[d.rRES[0], d.rRES[1]])
                self.DMA("sync", d.MU8[:, 0:n], ml[e], "msk", writes=[rM8])
                self.E("vector", lambda v, n=n: v.tensor_copy(d.XNEW[:, 0:n], d.MU8[:, 0:n]),
                       reads=[rM8], writes=[rMskf])
                self.E("vector", lambda v, n=n: v.tensor_mul(
                    d.XNEW[:, 0:n], d.RES[:, 0:n], d.XNEW[:, 0:n]),
                    reads=[d.rRES[0], d.rRES[1], rMskf], writes=[rMskf])
                src = d.XNEW[:, 0:n].rearrange("p (c r) -> p c r", r=R)
                dstv = w01v[:, :, roff:roff + R]
                if e == 0:
                    self.E("vector", lambda v, a=dstv, b=src: v.tensor_copy(a, b),
                           reads=[rMskf], writes=[rW01])
                else:
                    self.E("vector", lambda v, a=dstv, b=src: v.tensor_tensor(a, b, a, ALU.add),
                           reads=[rMskf, rW01], writes=[rW01])
                yield
        rENC = Reg("enc01")
        for hh in range(4):
            self.DMA("sync", d.RES[0:40, 0:1024], d.enc01_d[:, hh * 1024:(hh + 1) * 1024],
                     "enc", writes=[d.rRES[0], d.rRES[1]])
            self.E("vector", lambda v, hh=hh: v.tensor_copy(
                d.ENC01B[:, hh * 1024:(hh + 1) * 1024], d.RES[0:40, 0:1024]),
                reads=[d.rRES[0], d.rRES[1]], writes=[rENC])
            yield
        rWIN = Reg("wint")
        for c in range(CF):
            b = c % 2
            self.DMA("sync", d.RES[:, b * 512:b * 512 + 32],
                     d.WinT_d[c * 128:(c + 1) * 128, :], f"win{b}", writes=[d.rRES[b]])
            self.E("vector", lambda v, c=c, b=b: v.tensor_copy(
                d.WINTB[:, c * 32:(c + 1) * 32], d.RES[:, b * 512:b * 512 + 32]),
                reads=[d.rRES[b]], writes=[d.rCONST])
            yield
        rW01B = Reg("w01sb")
        self.E("vector", lambda v: v.tensor_copy(d.W01SB[:], d.W01S[:]),
               reads=[rW01], writes=[rW01B])
        rMBF = Reg("mbf")
        self.rMBF = rMBF
        rMAC = Reg("macc")
        self.E("vector", lambda v: v.memset(d.MACC[:], 0.0), writes=[rMAC])
        for n2 in range(4):
            for cc in range(2 * CF):
                c = cc % CF
                src = d.KM if cc < CF else d.KML
                buf = cc % 2
                sl32 = d.T1B[:, 0:1024] if False else d.XNEW[:, 0:1024]
                slbf = d.EXPAT[:, buf * 1024:(buf + 1) * 1024]
                self.DMA("sync", sl32, src[c * 128:(c + 1) * 128, n2 * 1024:(n2 + 1) * 1024],
                         f"km{buf}", writes=[d.rXNEW[0], d.rXNEW[1]])
                self.E("vector", lambda v, a=slbf, b=sl32: v.tensor_copy(a, b),
                       reads=[d.rXNEW[0], d.rXNEW[1]],
                       writes=[d.rEXP[buf * 2], d.rEXP[buf * 2 + 1]])
                def emitm(t, slbf=slbf, c=c):
                    t.matmul(self.PKS[0:40, 0:512],
                             d.W01SB[:, c * 40:(c + 1) * 40], slbf[:, 0:512],
                             start=True, stop=True)
                    return t.matmul(self.PKS[0:40, 512:1024],
                                    d.W01SB[:, c * 40:(c + 1) * 40], slbf[:, 512:1024],
                                    start=True, stop=True)
                self.E("tensor", emitm, reads=[rW01B, d.rEXP[buf * 2], d.rEXP[buf * 2 + 1]],
                       writes=[self.rPK[0], self.rPK[1]])
                self.E("vector", lambda v: v.tensor_add(
                    d.MACC[0:40, :], d.MACC[0:40, :], self.PKS[0:40, 0:1024]),
                    reads=[self.rPK[0], self.rPK[1], rMAC], writes=[rMAC])
                yield
            self.E("vector", lambda v, n2=n2: v.tensor_copy(
                d.MBF[:, n2 * 1024:(n2 + 1) * 1024], d.MACC[0:40, 0:1024]),
                reads=[rMAC], writes=[rMBF])
            self.E("vector", lambda v: v.memset(d.MACC[:], 0.0), writes=[rMAC])
            yield
        for fb in range(CF):
            for g in range(8):
                sl = g % 2
                res = d.RES[:, sl * 512:(sl + 1) * 512]
                self.DMA("sync", res, d.WbT_d[fb * 128:(fb + 1) * 128, g * 512:(g + 1) * 512],
                         f"wbt{sl}", writes=[d.rRES[sl]])
                pki = g % 2
                self.E("tensor", lambda t, fb=fb, g=g, i=pki: t.matmul(
                    self.pk(i), d.ENC01B[:, fb * 128:(fb + 1) * 128],
                    d.MBF[:, g * 512:(g + 1) * 512], start=True, stop=True),
                    reads=[rENC, rMBF], writes=[self.rPK[pki]])
                eb = g % 2
                ev = d.EVB[:, eb * 512:(eb + 1) * 512]
                self.E("vector", lambda v, o=ev, p=self.pk(pki), r=res: v.scalar_tensor_tensor(
                    o, p, 0.5, r, ALU.mult, ALU.add),
                    reads=[self.rPK[pki], d.rRES[sl]], writes=[d.rEVB[eb]])
                for q in range(4):
                    self.DMA("scalar", self.WeffB[g * 4 + q, :, fb * 128:(fb + 1) * 128],
                             ev[:, q * 128:(q + 1) * 128], f"wf{eb}",
                             reads=[d.rEVB[eb]], writes=[self.rWB["Weff"]])
                yield

    def ph_ffn(self, xsrc, ksrc, xdst, kdst, last):
        d = self
        stat_pk = {0: (2, 4), 1: (3, 5)}
        for h in range(2):
            rXM = self.bignew(1)[0]
            yield from self.ph_xm(ksrc, xsrc, h * 1024, 1024, 0, rXM)
            for mb in range(32):
                wb = mb % 2
                ws = d.WS[:, wb * 4096:(wb + 1) * 4096]
                self.DMA("sync", ws, self.WeffB[mb], f"ws{wb}",
                         reads=[self.rWB["Weff"]], writes=[d.rWS[wb]])
                def emit(t, ws=ws):
                    last_i = None
                    for c in range(CF):
                        t.matmul(self.pk(0), ws[:, c * 128:(c + 1) * 128],
                                 d.BIG[:, c * 1024: c * 1024 + 512],
                                 start=(c == 0), stop=(c == CF - 1))
                        last_i = t.matmul(self.pk(1), ws[:, c * 128:(c + 1) * 128],
                                          d.BIG[:, c * 1024 + 512: c * 1024 + 1024],
                                          start=(c == 0), stop=(c == CF - 1))
                    return last_i
                self.E("tensor", emit, reads=[d.rWS[wb], rXM],
                       writes=[self.rPK[0], self.rPK[1]])
                for ng in range(2):
                    rb = ng
                    res = d.RES[:, rb * 512:(rb + 1) * 512]
                    self.DMA("sync", res,
                             xsrc[mb * 128:(mb + 1) * 128, h * 1024 + ng * 512: h * 1024 + (ng + 1) * 512],
                             f"xr{rb}", reads=[self.xreg(ksrc, h * 2 + ng)], writes=[d.rRES[rb]])
                    xb = ng
                    xn = d.XNEW[:, xb * 512:(xb + 1) * 512]
                    self.E("vector", lambda v, o=xn, p=self.pk(ng), m=mb, r=res: v.scalar_tensor_tensor(
                        o, p, d.BBC[:, m:m + 1], r, ALU.add, ALU.add),
                        reads=[self.rPK[ng], d.rCONST, d.rRES[rb]], writes=[d.rXNEW[xb]])
                    self.DMA("scalar",
                             xdst[mb * 128:(mb + 1) * 128, h * 1024 + ng * 512: h * 1024 + (ng + 1) * 512],
                             xn, f"xw{xb}", reads=[d.rXNEW[xb]], writes=[self.xreg(kdst, h * 2 + ng)])
                    if not last:
                        eb = ng
                        xbf = d.EVB[:, eb * 512:(eb + 1) * 512]
                        sqb = d.EVB[:, 1024 + eb * 512:1024 + (eb + 1) * 512]
                        self.E("vector", lambda v, a=xbf, i=xn: v.tensor_copy(a, i),
                               reads=[d.rXNEW[xb]], writes=[d.rEVB[eb]])
                        self.E("scalar", lambda sc, a=sqb, i=xbf: sc.activation(a, i, AF.Square),
                               reads=[d.rEVB[eb]], writes=[d.rEVB[2 + eb]])
                        smb, sqk = stat_pk[ng]
                        self.E("tensor", lambda t, a=xbf, st=(mb == 0), z=(mb == 31), i=smb: t.matmul(
                            self.pk(i, (0, 1)), d.ONECB[:], a, start=st, stop=z),
                            reads=[d.rCONST, d.rEVB[eb]], writes=[self.rPK[smb]])
                        self.E("tensor", lambda t, a=sqb, st=(mb == 0), z=(mb == 31), i=sqk: t.matmul(
                            self.pk(i, (0, 1)), d.ONECB[:], a, start=st, stop=z),
                            reads=[d.rCONST, d.rEVB[2 + eb]], writes=[self.rPK[sqk]])
                yield
            if not last:
                if h == 0:
                    self.rMUB.turnover()
                    self.rRSB.turnover()
                for ng in range(2):
                    smb, sqk = stat_pk[ng]
                    self._stats_rows(h * 1024 + ng * 512, 512, smb, sqk)
                yield

    def ph_attn_b(self):
        d = self
        KOFF = 16384
        SL = lambda i: d.BIG[:, KOFF + i * 512: KOFF + (i + 1) * 512]
        rST = Reg("states")
        rROPE = Reg("rope")
        rDT = Reg("dtq"); rGG = Reg("ggq"); rG = Reg("gq"); rEG = Reg("egq")
        rED = Reg("edq"); rCD = Reg("cdq"); rGTB = Reg("gtb"); rGRW = Reg("growh")
        rSSH = [Reg(f"ssh{i}") for i in range(8)]; rRSH = [Reg(f"rsh{i}") for i in range(8)]
        rRST = Reg("rstt"); rRSR = Reg("rsrow")
        rDEC = Reg("dect"); rYS = Reg("ysb"); rSTB = Reg("stb"); rSD = Reg("sdt")
        rKD = Reg("kdt"); rXC = Reg("xct"); rYB = Reg("yb16"); rGLB = Reg("glb")
        rRP1 = d.rTRS[0]; rRP2 = Reg("rp2")
        self.E("vector", lambda v: v.memset(d.STATES[:], 0.0), writes=[rST])
        base = merge(*[r.w for r in self.big_live], *[r.r for r in self.big_live])
        rXMB = Reg("xmb", multi=True); rXMB.r = list(base)
        tr = {}
        for nm, cnt in (("kt", 2), ("vt", 2), ("ktok", 2), ("vtok", 2), ("qt", 2), ("yth", 8)):
            tr[nm] = [Reg(f"{nm}{i}", multi=True) for i in range(cnt)]
            for rg in tr[nm]:
                rg.r = list(base)
        self.big_live = [rXMB] + [rg for nm in tr for rg in tr[nm]]
        kt = lambda p: SL(0 + p); vt = lambda p: SL(2 + p)
        ktok = lambda p: SL(4 + p); vtok = lambda p: SL(6 + p)
        qts = lambda p: SL(8 + p); yth = lambda p: SL(10 + p)

        def gen_A(bi, h, xm):
            kvh = h // 4
            kp = kvh % 2
            hp = h % 2
            if h % 4 == 0:
                for (dstf, dreg, WB, wname, wb) in ((kt, tr["kt"][kp], self.WkB, "Wk", 0),
                                                    (vt, tr["vt"][kp], self.WvB, "Wv", 1)):
                    ws = d.WS[:, wb * 4096:(wb + 1) * 4096]
                    self.DMA("sync", ws, WB[kvh], f"ws{wb}",
                             reads=[self.rWB[wname]], writes=[d.rWS[wb]])
                    def emitp(t, ws=ws, xm=xm):
                        last_i = None
                        for c in range(CF):
                            last_i = t.matmul(self.pk(0), ws[:, c * 128:(c + 1) * 128],
                                              xm(c, 0, 512), start=(c == 0), stop=(c == CF - 1))
                        return last_i
                    self.E("tensor", emitp, reads=[d.rWS[wb], rXMB], writes=[self.rPK[0]])
                    self.E("vector", lambda v, dstf=dstf, kp=kp: v.tensor_copy(
                        dstf(kp), self.pk(0)), reads=[self.rPK[0]], writes=[dreg])
                    yield
                self._rope(kt(kp), 512, tr["kt"][kp], rROPE, rRP1, rRP2)
                yield
                for blk in range(4):
                    for (srcf, sreg, dstf, dreg) in ((kt, tr["kt"][kp], ktok, tr["ktok"][kp]),
                                                     (vt, tr["vt"][kp], vtok, tr["vtok"][kp])):
                        pb = blk % 2
                        pt = self.PTA if pb == 0 else self.PTB
                        self.E("tensor", lambda t, p=pt, srcf=srcf, kp=kp, blk=blk: t.transpose(
                            p[:, 0:128], srcf(kp)[:, blk * 128:(blk + 1) * 128], d.IDNB[:]),
                            reads=[sreg, d.rCONST], writes=[self.rPT[pb]])
                        self.E("vector", lambda v, p=pt, dstf=dstf, kp=kp, blk=blk: v.tensor_copy(
                            dstf(kp)[:, blk * 128:(blk + 1) * 128], p[:, 0:128]),
                            reads=[self.rPT[pb]], writes=[dreg])
                    yield
            wb = hp
            ws = d.WS[:, wb * 4096:(wb + 1) * 4096]
            self.DMA("sync", ws, self.WqB[h], f"ws{wb}",
                     reads=[self.rWB["Wq"]], writes=[d.rWS[wb]])
            def emitq(t, ws=ws, xm=xm):
                last_i = None
                for c in range(CF):
                    last_i = t.matmul(self.pk(1), ws[:, c * 128:(c + 1) * 128],
                                      xm(c, 0, 512), start=(c == 0), stop=(c == CF - 1))
                return last_i
            self.E("tensor", emitq, reads=[d.rWS[wb], rXMB], writes=[self.rPK[1]])
            self.E("vector", lambda v, hp=hp: v.tensor_copy(qts(hp), self.pk(1)),
                   reads=[self.rPK[1]], writes=[tr["qt"][hp]])
            yield
            self._rope(qts(hp), 512, tr["qt"][hp], rROPE, rRP1, rRP2)
            yield

        def gen_B(bi, h):
            kvh = h // 4
            kp = kvh % 2
            hp = h % 2
            self.DMA("sync", d.GROWH[0:1, 0:512], d.GTB[h:h + 1, 0:512],
                     "grw", reads=[rGTB], writes=[rGRW])
            st_ap = d.STATES[:, h * 128:(h + 1) * 128]
            for blk in range(4):
                self.E("vector", lambda v, st=st_ap: v.tensor_copy(d.STB[:], st),
                       reads=[rST], writes=[rSTB])
                self.E("tensor", lambda t, kp=kp, hp=hp, blk=blk: t.matmul(
                    self.pk(2)[:, 0:128], kt(kp)[:, blk * 128:(blk + 1) * 128],
                    qts(hp)[:, blk * 128:(blk + 1) * 128], start=True, stop=True),
                    reads=[tr["kt"][kp], tr["qt"][hp]], writes=[self.rPK[2]])
                self.E("tensor", lambda t, blk=blk: t.matmul(
                    self.pk(3)[:, 0:128], d.ONE1F[:],
                    d.GROWH[0:1, blk * 128:(blk + 1) * 128], start=True, stop=True),
                    reads=[rGRW, d.rCONST], writes=[self.rPK[3]])
                gcol = d.GQ[:, blk * 32 + h:blk * 32 + h + 1] if False else \
                    d.GQ[:, blk * 32 + h:blk * 32 + h + 1]
                gcol = d.GQ[:, blk * 32 + h:blk * 32 + h + 1]
                self.E("vector", lambda v, g=gcol: v.scalar_tensor_tensor(
                    d.DECT[:], self.pk(3)[:, 0:128], g, d.BIAST[:],
                    ALU.subtract, ALU.add),
                    reads=[self.rPK[3], rG, d.rCONST], writes=[rDEC])
                self.E("scalar", lambda s: s.activation(d.DECT[:], d.DECT[:], AF.Exp),
                       reads=[rDEC], writes=[rDEC])
                self.E("vector", lambda v: v.tensor_tensor(d.SDT[:], self.pk(2)[:, 0:128],
                                                           d.DECT[:], ALU.mult),
                       reads=[self.rPK[2], rDEC], writes=[rSD])
                edcol = d.EDQ[:, blk * 32 + h:blk * 32 + h + 1]
                dtcol = d.DTQ[:, blk * 32 + h:blk * 32 + h + 1]
                self.E("vector", lambda v, e=edcol, kp=kp, blk=blk: v.tensor_scalar(
                    d.KDT[:], ktok(kp)[:, blk * 128:(blk + 1) * 128], e, None, ALU.mult),
                    reads=[tr["ktok"][kp], rED], writes=[rKD])
                self.E("vector", lambda v, e=dtcol, kp=kp, blk=blk: v.tensor_scalar(
                    d.XCT[:], vtok(kp)[:, blk * 128:(blk + 1) * 128], e, None, ALU.mult),
                    reads=[tr["vtok"][kp], rDT], writes=[rXC])
                def emity(t, hp=hp, blk=blk):
                    t.matmul(self.pk(4)[:, 0:128], d.SDT[:], d.XCT[:],
                             start=True, stop=True)
                    return t.matmul(self.pk(4)[:, 128:256],
                                    qts(hp)[:, blk * 128:(blk + 1) * 128], d.STB[:],
                                    start=True, stop=True)
                self.E("tensor", emity, reads=[rSD, rXC, tr["qt"][hp], rSTB],
                       writes=[self.rPK[4]])
                self.E("tensor", lambda t: t.matmul(self.pk(5)[:, 0:128], d.KDT[:],
                                                    d.XCT[:], start=True, stop=True),
                       reads=[rKD, rXC], writes=[self.rPK[5]])
                egcol = d.EGQ[:, blk * 32 + h:blk * 32 + h + 1]
                self.E("vector", lambda v, e=egcol: v.tensor_scalar(
                    d.YSB[:], self.pk(4)[:, 128:256], e, None, ALU.mult),
                    reads=[self.rPK[4], rEG], writes=[rYS])
                self.E("vector", lambda v: v.tensor_tensor(d.YSB[:], d.YSB[:],
                                                           self.pk(4)[:, 0:128], ALU.add),
                       reads=[self.rPK[4], rYS], writes=[rYS])
                sl8 = h % 8
                self.E("scalar", lambda s, sl8=sl8, blk=blk: s.activation(
                    d.TRS2[:], d.YSB[:], AF.Square,
                    accum_out=d.SSH[:, sl8 * 4 + blk:sl8 * 4 + blk + 1]),
                    reads=[rYS], writes=[rSSH[sl8]])
                self.E("vector", lambda v: v.tensor_copy(d.YB16[:], d.YSB[:]),
                       reads=[rYS], writes=[rYB])
                pb = blk % 2
                pt = self.PTA if pb == 0 else self.PTB
                self.E("tensor", lambda t, p=pt: t.transpose(p[:, 0:128], d.YB16[:],
                                                             d.IDNB[:]),
                       reads=[rYB, d.rCONST], writes=[self.rPT[pb]])
                self.E("vector", lambda v, p=pt, sl8=(h % 8), blk=blk: v.tensor_copy(
                    yth(sl8)[:, blk * 128:(blk + 1) * 128], p[:, 0:128]),
                    reads=[self.rPT[pb]], writes=[tr["yth"][h % 8]])
                cdcol = d.CDQ[:, blk * 32 + h:blk * 32 + h + 1]
                self.E("vector", lambda v, c=cdcol, st=st_ap: v.scalar_tensor_tensor(
                    st, st, c, self.pk(5)[:, 0:128], ALU.mult, ALU.add),
                    reads=[rST, rCD, self.rPK[5]], writes=[rST])
                yield

        def gen_C4(bi, h0, t0, xm):
            for hh in range(h0, h0 + 4):
                sl8 = hh % 8
                self.E("scalar", lambda s, sl8=sl8: s.activation(
                    d.RSTDH[:, sl8 * 4:sl8 * 4 + 4], d.SSH[:, sl8 * 4:sl8 * 4 + 4], AF.Ln,
                    scale=1.0 / 128.0, bias=d.EPSB[:]),
                    reads=[rSSH[sl8], d.rCONST], writes=[rRSH[sl8]])
                self.E("scalar", lambda s, sl8=sl8: s.activation(
                    d.RSTDH[:, sl8 * 4:sl8 * 4 + 4], d.RSTDH[:, sl8 * 4:sl8 * 4 + 4], AF.Exp,
                    scale=-0.5),
                    reads=[rRSH[sl8]], writes=[rRSH[sl8]])
                yield
            for hh in range(h0, h0 + 4):
                sl8 = hh % 8
                self.E("tensor", lambda t, sl8=sl8: t.transpose(
                    self.pk(2, (0, 4))[:, 0:128], d.RSTDH[:, sl8 * 4:sl8 * 4 + 4], d.IDN[:]),
                    reads=[rRSH[sl8], d.rCONST], writes=[self.rPK[2]])
                self.E("vector", lambda v: v.tensor_copy(d.RSTT[0:4, :],
                                                         self.pk(2, (0, 4))[:, 0:128]),
                       reads=[self.rPK[2]], writes=[rRST])
                self.DMA("gpsimd", d.RSROW[0:1, 0:512], d.RSTT[0:4, :], "rsw",
                         reads=[rRST], writes=[rRSR])
                wb = hh % 2
                ws = d.WS[:, wb * 4096:(wb + 1) * 4096]
                self.DMA("sync", ws, self.WgB[hh], f"ws{wb}",
                         reads=[self.rWB["Wg"]], writes=[d.rWS[wb]])
                def emitg(t, ws=ws, xm=xm):
                    last_i = None
                    for c in range(CF):
                        last_i = t.matmul(self.pk(0), ws[:, c * 128:(c + 1) * 128],
                                          xm(c, 0, 512), start=(c == 0), stop=(c == CF - 1))
                    return last_i
                self.E("tensor", emitg, reads=[d.rWS[wb], rXMB], writes=[self.rPK[0]])
                self.E("scalar", lambda s: s.activation(d.TRS[:, 0:512], self.pk(0), AF.Silu),
                       reads=[self.rPK[0]], writes=[rRP1])
                self.E("vector", lambda v: v.tensor_scalar(
                    d.TRS[:, 0:512], d.TRS[:, 0:512], d.GN[:], None, ALU.mult),
                    reads=[rRP1, d.rCONST], writes=[rRP1])
                self.E("tensor", lambda t: t.matmul(
                    self.pk(3), d.ONE1F[:], d.RSROW[0:1, 0:512], start=True, stop=True),
                    reads=[rRSR, d.rCONST], writes=[self.rPK[3]])
                self.E("vector", lambda v, sl8=sl8: v.tensor_tensor(
                    d.RP2[:, 0:512], yth(sl8)[:, 0:512], self.pk(3), ALU.mult),
                    reads=[tr["yth"][sl8], self.rPK[3]], writes=[rRP2])
                self.E("vector", lambda v: v.tensor_tensor(
                    d.RP2[:, 0:512], d.RP2[:, 0:512], d.TRS[:, 0:512], ALU.mult),
                    reads=[rRP1, rRP2], writes=[rRP2])
                self.DMA("scalar", self.yTb[hh, :, t0:t0 + 512], d.RP2[:, 0:512],
                         f"yb{hh % 2}", reads=[rRP2], writes=[self.rYTb])
                yield

        blocks = [(0, 512), (512, 512), (1024, 512), (1536, 512)]
        for bi, (t0, Tb) in enumerate(blocks):
            rXMB.turnover()
            yield from self.ph_xm(2, self.xT2, t0, Tb, 0, rXMB)
            xm = lambda c, a, w, Tb=Tb: d.BIG[:, c * Tb + a: c * Tb + a + w]
            self.DMA("sync", d.ROPEC[:, 0:Tb], d.cosT_d[:, t0:t0 + Tb], "rpc", writes=[rROPE])
            self.DMA("sync", d.ROPES[:, 0:Tb], d.sinTs_d[:, t0:t0 + Tb], "rps", writes=[rROPE])
            for blk in range(4):
                def emit(t, blk=blk, xm=xm):
                    last_i = None
                    for c in range(CF):
                        last_i = t.matmul(self.pk(4)[:, 0:32], xm(c, blk * 128, 128),
                                          d.WINTB[:, c * 32:(c + 1) * 32],
                                          start=(c == 0), stop=(c == CF - 1))
                    return last_i
                self.E("tensor", emit, reads=[rXMB, d.rCONST], writes=[self.rPK[4]])
                dtr = d.DTQ[:, blk * 32:(blk + 1) * 32]
                self.E("vector", lambda v, o=dtr: v.tensor_tensor(o, self.pk(4)[:, 0:32],
                                                                  d.DTBB[:, 0:32], ALU.add),
                       reads=[self.rPK[4], d.rCONST], writes=[rDT])
                a1 = d.YSB[:, 0:32]; a2 = d.YSB[:, 32:64]; a3 = d.YSB[:, 64:96]
                self.E("vector", lambda v, i=dtr: v.scalar_tensor_tensor(
                    a1, i, -1.0, i, ALU.mult, ALU.max), reads=[rDT], writes=[rYS])
                self.E("scalar", lambda s: s.activation(a2, a1, AF.Exp, scale=-1.0),
                       reads=[rYS], writes=[rYS])
                self.E("scalar", lambda s: s.activation(a2, a2, AF.Ln, bias=1.0),
                       reads=[rYS], writes=[rYS])
                self.E("vector", lambda v, i=dtr: v.tensor_scalar_max(a3, i, 0.0),
                       reads=[rDT], writes=[rYS])
                self.E("vector", lambda v, o=dtr: v.tensor_add(o, a2, a3),
                       reads=[rYS], writes=[rDT])
                gg = d.GGQ[:, blk * 32:(blk + 1) * 32]
                self.E("vector", lambda v, o=gg, i=dtr: v.tensor_tensor(o, i, d.ABB[:, 0:32],
                                                                        ALU.mult),
                       reads=[rDT, d.rCONST], writes=[rGG])
                self.E("tensor", lambda t, i=gg: t.matmul(self.pk(5)[:, 0:32], d.UTRI[:], i,
                                                          start=True, stop=True),
                       reads=[rGG, d.rCONST], writes=[self.rPK[5]])
                gq = d.GQ[:, blk * 32:(blk + 1) * 32]
                self.E("vector", lambda v, o=gq: v.tensor_copy(o, self.pk(5)[:, 0:32]),
                       reads=[self.rPK[5]], writes=[rG])
                eg = d.EGQ[:, blk * 32:(blk + 1) * 32]
                self.E("scalar", lambda s, o=eg, i=gq: s.activation(o, i, AF.Exp),
                       reads=[rG], writes=[rEG])
                self.E("tensor", lambda t, i=gq: t.transpose(
                    self.pk(2, (0, 32))[:, 0:128], i, d.IDN[:]),
                    reads=[rG, d.rCONST], writes=[self.rPK[2]])
                self.E("vector", lambda v, blk=blk: v.tensor_copy(
                    d.GTB[0:32, blk * 128:(blk + 1) * 128], self.pk(2, (0, 32))[:, 0:128]),
                    reads=[self.rPK[2]], writes=[rGTB])
                self.E("tensor", lambda t, blk=blk: t.transpose(
                    self.pk(3, (0, 1))[:, 0:32],
                    d.GTB[0:32, blk * 128 + 127:blk * 128 + 128], d.IDN[0:32, 0:32]),
                    reads=[rGTB, d.rCONST], writes=[self.rPK[3]])
                self.E("vector", lambda v: v.tensor_copy(d.SROW[0:1, 1536:1568],
                                                         self.pk(3, (0, 1))[:, 0:32]),
                       reads=[self.rPK[3]], writes=[d.rSROW[3]])
                self.E("tensor", lambda t: t.matmul(self.pk(3)[:, 0:32], d.ONE1F[:],
                                                    d.SROW[0:1, 1536:1568],
                                                    start=True, stop=True),
                       reads=[d.rSROW[3], d.rCONST], writes=[self.rPK[3]])
                self.E("vector", lambda v: v.tensor_copy(d.GLB[:], self.pk(3)[:, 0:32]),
                       reads=[self.rPK[3]], writes=[rGLB])
                ed = d.EDQ[:, blk * 32:(blk + 1) * 32]
                self.E("vector", lambda v, o=ed, i=gq: v.tensor_tensor(o, d.GLB[:], i,
                                                                       ALU.subtract),
                       reads=[rGLB, rG], writes=[rED])
                self.E("scalar", lambda s, o=ed: s.activation(o, o, AF.Exp),
                       reads=[rED], writes=[rED])
                cd = d.CDQ[:, blk * 32:(blk + 1) * 32]
                self.E("scalar", lambda s, o=cd: s.activation(o, d.GLB[:], AF.Exp),
                       reads=[rGLB], writes=[rCD])
                yield
            for _ in gen_A(bi, 0, xm):
                yield
            pendC = iter(())
            for h in range(32):
                ga = gen_A(bi, h + 1, xm) if h + 1 < 32 else iter(())
                gb = gen_B(bi, h)
                b_alive = a_alive = True
                while b_alive or a_alive:
                    if b_alive:
                        try:
                            next(gb)
                        except StopIteration:
                            b_alive = False
                    if a_alive:
                        try:
                            next(ga)
                        except StopIteration:
                            a_alive = False
                    try:
                        next(pendC)
                    except StopIteration:
                        pass
                    yield
                if h % 4 == 3:
                    import itertools as _it
                    pendC = _it.chain(pendC, gen_C4(bi, h - 3, t0, xm))
            for _ in pendC:
                yield
            rSLAB = self.bignew(1)[0]
            self.big_live = [rSLAB, rXMB] + [rg for nm in tr for rg in tr[nm]]
            for c in range(CF):
                self.DMA("sync", d.BIG[:, c * Tb:(c + 1) * Tb], self.yTb[c, :, t0:t0 + Tb],
                         f"ybl{bi}", reads=[self.rYTb], writes=[rSLAB])
            for mb in range(32):
                wb = mb % 2
                ws = d.WS[:, wb * 4096:(wb + 1) * 4096]
                self.DMA("sync", ws, self.WoB[mb], f"ws{wb}",
                         reads=[self.rWB["Wo"]], writes=[d.rWS[wb]])
                pko = 1 if mb % 2 == 0 else 3
                def emito(t, ws=ws, Tb=Tb, pko=pko):
                    last_i = None
                    for c in range(CF):
                        last_i = t.matmul(self.pk(pko), ws[:, c * 128:(c + 1) * 128],
                                          d.BIG[:, c * Tb: c * Tb + 512],
                                          start=(c == 0), stop=(c == CF - 1))
                    return last_i
                self.E("tensor", emito, reads=[d.rWS[wb], rSLAB], writes=[self.rPK[pko]])
                rb = mb % 2
                res = d.RES[:, rb * 512:(rb + 1) * 512]
                self.DMA("sync", res, self.xT2[mb * 128:(mb + 1) * 128, t0:t0 + 512],
                         f"xr{rb}", reads=[self.xreg(2, t0 // 512)], writes=[d.rRES[rb]])
                xb = mb % 2
                xn = d.XNEW[:, xb * 512:(xb + 1) * 512]
                self.E("vector", lambda v, o=xn, r=res, pko=pko: v.tensor_tensor(
                    o, self.pk(pko), r, ALU.add),
                    reads=[self.rPK[pko], d.rRES[rb]], writes=[d.rXNEW[xb]])
                self.DMA("scalar", self.xT3[mb * 128:(mb + 1) * 128, t0:t0 + 512],
                         xn, f"xw{xb}", reads=[d.rXNEW[xb]],
                         writes=[self.xreg(3, t0 // 512)])
                eb = mb % 2
                xbf = d.EVB[:, eb * 512:(eb + 1) * 512]
                sqb = d.EVB[:, 1024 + eb * 512:1024 + (eb + 1) * 512]
                self.E("vector", lambda v, a=xbf, i=xn: v.tensor_copy(a, i),
                       reads=[d.rXNEW[xb]], writes=[d.rEVB[eb]])
                self.E("scalar", lambda sc, a=sqb, i=xbf: sc.activation(a, i, AF.Square),
                       reads=[d.rEVB[eb]], writes=[d.rEVB[2 + eb]])
                self.E("tensor", lambda t, a=xbf, st=(mb == 0), z=(mb == 31): t.matmul(
                    self.pk(2, (0, 1)), d.ONECB[:], a, start=st, stop=z),
                    reads=[d.rCONST, d.rEVB[eb]], writes=[self.rPK[2]])
                self.E("tensor", lambda t, a=sqb, st=(mb == 0), z=(mb == 31): t.matmul(
                    self.pk(4, (0, 1)), d.ONECB[:], a, start=st, stop=z),
                    reads=[d.rCONST, d.rEVB[2 + eb]], writes=[self.rPK[4]])
                yield
            if bi == 0:
                self.rMUB.turnover()
                self.rRSB.turnover()
            self._stats_rows(t0, 512, 2, 4)
            yield


    def _rope(self, buf, Tb, rbuf, rROPE, rRP1, rRP2):
        d = self
        self.E("gpsimd", lambda g: g.tensor_copy(d.RP2[0:64, 0:Tb], buf[64:128, 0:Tb]),
               reads=[rbuf], writes=[rRP2])
        self.E("gpsimd", lambda g: g.tensor_copy(d.RP2[64:128, 0:Tb], buf[0:64, 0:Tb]),
               reads=[rbuf], writes=[rRP2])
        self.E("gpsimd", lambda g: g.tensor_mul(d.TRS[:, 0:Tb], buf[:, 0:Tb],
                                                d.ROPEC[:, 0:Tb]),
               reads=[rbuf, rROPE], writes=[rRP1])
        self.E("gpsimd", lambda g: g.tensor_mul(d.RP2[:, 0:Tb], d.RP2[:, 0:Tb],
                                                d.ROPES[:, 0:Tb]),
               reads=[rRP2, rROPE], writes=[rRP2])
        self.E("gpsimd", lambda g: g.tensor_add(buf[:, 0:Tb], d.TRS[:, 0:Tb],
                                                d.RP2[:, 0:Tb]),
               reads=[rRP1, rRP2], writes=[rbuf])


def build(stages=4, dbg_hook=None):
    import itertools
    b = B(stages)
    b.ph_consts()
    side_parts = [b.ph_cast(b.WqkvL, b.WqkvB, 24, DM, 0, b.rWB["Wqkv"]),
                  b.ph_cast(b.WoaL, b.WoaB, 32, 1024, 1, b.rWB["Woa"])]
    if stages >= 3:
        side_parts += [b.ph_cast(b.WqL, b.WqB, 32, DM, 2, b.rWB["Wq"]),
                       b.ph_cast(b.WkL, b.WkB, 8, DM, 3, b.rWB["Wk"]),
                       b.ph_cast(b.WvL, b.WvB, 8, DM, 4, b.rWB["Wv"])]
    if stages >= 2:
        side_parts.append(b.ph_prep())
    if stages >= 3:
        side_parts += [b.ph_cast(b.WgL, b.WgB, 32, DM, 5, b.rWB["Wg"]),
                       b.ph_cast(b.WoL, b.WoB, 32, DM, 6, b.rWB["Wo"])]
    side = itertools.chain(*side_parts)
    main_parts = [b.ph_attn_a(), b.ph_attn_a_out()]
    if stages >= 2:
        main_parts.append(b.ph_ffn(b.xT1, 1, b.xT2, 2, last=(stages == 2)))
    if stages >= 3:
        main_parts.append(b.ph_attn_b())
    if stages >= 4:
        main_parts.append(b.ph_ffn(b.xT3, 3, b.yT, 4, last=True))
    main = itertools.chain(*main_parts)
    while True:
        try:
            next(main)
        except StopIteration:
            break
        for _ in range(3):
            try:
                next(side)
            except StopIteration:
                break
    for _ in side:
        pass
    if dbg_hook is not None:
        dbg_hook(b)
    b.finish()
    b.w.emit()
    return b

def kernel(**inputs):
    hp = host_prepare(inputs)
    x = np.asarray(inputs["L0_attn_a_x"], dtype=np.float32)
    b = build(4)
    in_maps = []
    for core in range(8):
        im = dict(hp)
        im["xT0"] = np.ascontiguousarray(x[core].T)
        in_maps.append(im)
    res = run_bass_kernel_spmd(b.nc, in_maps, core_ids=list(range(8)))
    out = np.stack([np.asarray(r["yT"], dtype=np.float32).T for r in res.results])
    return np.ascontiguousarray(out)


# version 07 (final)
# speedup vs baseline: 1.0549x; 1.0549x over previous
import math
import contextlib
import numpy as np
import concourse.bass as bass
import concourse.mybir as mybir
from concourse.bass_utils import run_bass_kernel_spmd

F32 = mybir.dt.float32
BF16 = mybir.dt.bfloat16
U8 = mybir.dt.uint8
AF = mybir.ActivationFunctionType
ALU = mybir.AluOpType
AX = mybir.AxisListType

S = 2048
DM = 4096
CF = 32
LNEPS = 1e-5
RMSEPS = 1e-5


def _k_matrix():
    n = DM
    t = np.arange(n)
    K = np.empty((n, n), dtype=np.float64)
    K[0] = 1.0 / math.sqrt(n)
    f = np.arange(1, (n - 1) // 2 + 1)[:, None]
    ang = 2 * np.pi * f * t[None, :] / n
    K[1:n - 1:2] = np.cos(ang) / math.sqrt(n / 2.0)
    K[2:n:2] = np.sin(ang) / math.sqrt(n / 2.0)
    K[n - 1] = ((-1.0) ** t) / math.sqrt(n)
    hi = K.astype(np.float32)
    lo = (K - hi.astype(np.float64)).astype(np.float32)
    return hi, lo

def _rope_tables():
    D = 128
    try:
        import jax
        import jax.numpy as jnp
        cpu = jax.devices("cpu")[0]
        with jax.default_device(cpu):
            inv_freq = 1.0 / 10000.0 ** (jnp.arange(0, D, 2, dtype=jnp.float32) / D)
            ang = jnp.arange(S, dtype=jnp.float32)[:, None] * inv_freq
            cosj = np.asarray(jnp.cos(ang)).astype(np.float32)
            sinj = np.asarray(jnp.sin(ang)).astype(np.float32)
    except Exception:
        inv = (np.float32(1.0) / (np.float32(10000.0) **
               (np.arange(0, D, 2, dtype=np.float32) / np.float32(D))))
        ang = (np.arange(S, dtype=np.float32)[:, None] * inv).astype(np.float32)
        cosj = np.cos(ang).astype(np.float32)
        sinj = np.sin(ang).astype(np.float32)
    cos = np.concatenate([cosj, cosj], axis=-1)
    sin = np.concatenate([sinj, sinj], axis=-1)
    cosT = np.ascontiguousarray(cos.T).astype(np.float32)
    sinT = np.ascontiguousarray(sin.T).astype(np.float32)
    sinT[0:64] *= -1.0
    return cosT, sinT

def _strips_T(W, nmb):
    fout, fin = W.shape
    cf = fin // 128
    assert fout == nmb * 128
    return np.ascontiguousarray(
        W.reshape(nmb, 128, cf, 128).transpose(0, 3, 2, 1).reshape(nmb, 128, fin))

def _strips_N(W, nmb):
    fin, fout = W.shape
    cf = fin // 128
    assert fout == nmb * 128
    return np.ascontiguousarray(
        W.reshape(cf, 128, nmb, 128).transpose(2, 1, 0, 3).reshape(nmb, 128, fin))

def host_prepare(inputs):
    g = lambda k: np.asarray(inputs[k], dtype=np.float32)
    d = {}
    d["WqkvL"] = _strips_N(g("L0_attn_a_Wqkv"), 24)
    d["bqkv_c"] = np.ascontiguousarray(g("L0_attn_a_bqkv").reshape(24, 128).T)
    d["WoaL"] = _strips_N(g("L0_attn_a_Wo"), 32)
    d["bo_c"] = np.ascontiguousarray(g("L0_attn_a_bo").reshape(32, 128).T)
    d["WbT"] = np.ascontiguousarray(g("L0_ffn_W_base").T)
    d["bb_c"] = np.ascontiguousarray(g("L0_ffn_b_base").reshape(32, 128).T)
    d["enc01"] = np.ascontiguousarray(
        np.concatenate([g("L0_ffn_enc0"), g("L0_ffn_enc1")], axis=0))
    d["KM"], d["KML"] = _k_matrix()
    m0 = np.asarray(inputs["L0_ffn_mask0"]).astype(np.uint8)
    m1 = np.asarray(inputs["L0_ffn_mask1"]).astype(np.uint8)
    lay0 = lambda a: np.ascontiguousarray(a.reshape(3, 32, 128, 32).transpose(0, 2, 1, 3).reshape(3, 128, 1024))
    lay1 = lambda a: np.ascontiguousarray(a.reshape(3, 32, 128, 8).transpose(0, 2, 1, 3).reshape(3, 128, 256))
    d["c0L"] = lay0(g("L0_ffn_coefs0")); d["m0L"] = lay0(m0)
    d["c1L"] = lay1(g("L0_ffn_coefs1")); d["m1L"] = lay1(m1)
    d["WqL"] = _strips_T(g("L0_attn_b_Wq"), 32)
    d["WkL"] = _strips_T(g("L0_attn_b_Wk"), 8)
    d["WvL"] = _strips_T(g("L0_attn_b_Wv"), 8)
    d["WgL"] = _strips_T(g("L0_attn_b_Wg"), 32)
    d["WoL"] = _strips_T(g("L0_attn_b_Wo"), 32)
    d["WinT"] = np.ascontiguousarray(g("L0_attn_b_Win").T)
    d["dtb_r"] = g("L0_attn_b_dt_bias").reshape(1, 32).copy()
    d["alb_r"] = g("L0_attn_b_A_log_bias").reshape(1, 32).copy()
    d["gn_c"] = g("L0_attn_b_gnorm_w").reshape(128, 1).copy()
    cosT, sinT = _rope_tables()
    d["cosT"] = cosT; d["sinTs"] = sinT
    d["IDN"] = np.eye(128, dtype=np.float32)
    d["UTRI"] = np.triu(np.ones((128, 128), np.float32))
    ii = np.arange(128)
    d["BIAST"] = np.where(ii[None, :] >= ii[:, None], 0.0, -1e30).astype(np.float32)
    d["ONE1F"] = np.ones((1, 128), np.float32)
    return d


ENGINES = ("sync", "tensor", "vector", "scalar", "gpsimd")
ENG_SEM = {"tensor": "pe", "vector": "dv", "scalar": "ac", "gpsimd": "po", "sync": "sy"}

def merge(*deplists):
    best = {}
    for dl in deplists:
        for (s, v) in dl:
            if v > best.get(s, 0):
                best[s] = v
    return list(best.items())

class Reg:
    __slots__ = ("w", "r", "multi", "name")
    def __init__(self, name="", multi=False):
        self.w = []
        self.r = []
        self.multi = multi
        self.name = name
    def turnover(self):
        base = merge(self.w, self.r)
        self.w = []
        self.r = base
        return base

class Walk:
    def __init__(self, nc, es):
        self.nc = nc
        self.es = es
        self.steps = []
        self.sems = {}
        self.cnt = {}

    def sem(self, name):
        if name not in self.sems:
            self.sems[name] = self.es.enter_context(self.nc.semaphore(name))
            self.cnt[name] = 0
        return self.sems[name]

    def op(self, eng, fn, wait=(), inc=()):
        wait = [(s, v) for (s, v) in wait if v > 0]
        for s, n in inc:
            self.sem(s)
            self.cnt[s] += n
        for s, _ in wait:
            self.sem(s)
        self.steps.append((eng, fn, tuple(wait), tuple(inc)))

    def now(self, s):
        self.sem(s)
        return self.cnt[s]

    def emit(self):
        with self.nc.Block() as block:
            for engname in ENGINES:
                mysteps = [st for st in self.steps if st[0] == engname]
                if not mysteps:
                    continue
                def body(e, mysteps=mysteps):
                    lastw = {}
                    for (_, fn, waits, incs) in mysteps:
                        for (s, v) in waits:
                            if v > lastw.get(s, 0):
                                e.wait_ge(self.sems[s], v)
                                lastw[s] = v
                        inst = fn(e)
                        for (s, n) in incs:
                            inst.then_inc(self.sems[s], n)
                getattr(block, engname)(body)


class B:
    def __init__(self, stages=4):
        self.es = contextlib.ExitStack()
        self.stages = stages
        self.nc = bass.Bass("TRN2", target_bir_lowering=False)
        self.w = Walk(self.nc, self.es)
        self.dbg_outs = {}
        self._declare_dram()
        self._alloc_sbuf()
        self._alloc_psum()
        self._regions()

    def E(self, eng, fn, reads=(), writes=(), extra=(), stream=None):
        wl = [list(extra)]
        for r in reads:
            wl.append(r.w)
        for wr in writes:
            wl.append(wr.r)
            if not wr.multi:
                wl.append(wr.w)
        waits = merge(*wl)
        if eng == "tensor":
            waits = [(s, v) for (s, v) in waits if s != "pe"]
        sem = stream or ENG_SEM[eng]
        amt = 16 if stream else 1
        self.w.op(eng, fn, wait=waits, inc=[(sem, amt)])
        dep = [(sem, self.w.now(sem))]
        for r in reads:
            r.r = merge(r.r, dep)
        for wr in writes:
            if wr.multi:
                wr.w = merge(wr.w, dep)
            else:
                wr.w = dep
                wr.r = []
        return dep

    def DMA(self, eng, out, in_, stream, reads=(), writes=(), extra=()):
        return self.E(eng, lambda e, o=out, i=in_: e.dma_start(out=o, in_=i),
                      reads=reads, writes=writes, extra=extra, stream=stream)

    def _declare_dram(self):
        nc = self.nc
        I = lambda n, s, dt=F32: nc.dram_tensor(n, s, dt, kind="ExternalInput").ap()
        T = lambda n, s, dt=F32: nc.dram_tensor(n, s, dt, kind="Internal").ap()
        self.xT0 = I("xT0", [DM, S])
        self.WqkvL = I("WqkvL", [24, 128, DM]); self.bqkv_c = I("bqkv_c", [128, 24])
        self.WoaL = I("WoaL", [32, 128, 1024]); self.bo_c = I("bo_c", [128, 32])
        self.WbT_d = I("WbT", [DM, DM]); self.bb_c = I("bb_c", [128, 32])
        self.enc01_d = I("enc01", [40, DM]); self.KM = I("KM", [DM, DM])
        self.KML = I("KML", [DM, DM])
        self.c0L = I("c0L", [3, 128, 1024]); self.m0L = I("m0L", [3, 128, 1024], U8)
        self.c1L = I("c1L", [3, 128, 256]); self.m1L = I("m1L", [3, 128, 256], U8)
        self.WqL = I("WqL", [32, 128, DM]); self.WkL = I("WkL", [8, 128, DM])
        self.WvL = I("WvL", [8, 128, DM]); self.WgL = I("WgL", [32, 128, DM])
        self.WoL = I("WoL", [32, 128, DM]); self.WinT_d = I("WinT", [DM, 32])
        self.dtb_r = I("dtb_r", [1, 32]); self.alb_r = I("alb_r", [1, 32])
        self.gn_c = I("gn_c", [128, 1])
        self.cosT_d = I("cosT", [128, S]); self.sinTs_d = I("sinTs", [128, S])
        self.IDN_d = I("IDN", [128, 128]); self.UTRI_d = I("UTRI", [128, 128])
        self.BIAST_d = I("BIAST", [128, 128]); self.ONE1F_d = I("ONE1F", [1, 128])
        self.xT1 = T("xT1", [DM, S]); self.xT2 = T("xT2", [DM, S]); self.xT3 = T("xT3", [DM, S])
        self.WqkvB = T("WqkvB", [24, 128, DM], BF16)
        self.WoaB = T("WoaB", [32, 128, 1024], BF16)
        self.WeffB = T("WeffB", [32, 128, DM], BF16)
        self.WqB = T("WqB", [32, 128, DM], BF16); self.WkB = T("WkB", [8, 128, DM], BF16)
        self.WvB = T("WvB", [8, 128, DM], BF16); self.WgB = T("WgB", [8 * 4, 128, DM], BF16)
        self.WoB = T("WoB", [32, 128, DM], BF16)
        self.qTa = T("qTa", [8, 128, S], BF16); self.kTa = T("kTa", [8, 128, S], BF16)
        self.vta = T("vta", [16, 128, 1024], BF16); self.avTa = T("avTa", [8, 128, S], BF16)
        self.yTb = T("yTb", [32, 128, S], BF16)
        self.yT = nc.dram_tensor("yT", [DM, S], F32, kind="ExternalOutput").ap()

    def dbg(self, name, shape, dt=F32):
        ap = self.nc.dram_tensor(name, shape, dt, kind="ExternalOutput").ap()
        self.dbg_outs[name] = ap
        return ap

    def _alloc_sbuf(self):
        nc = self.nc
        RES_BYTES = 204800
        resv = nc.alloc_sbuf_tensor("RESV", [128, RES_BYTES], U8)
        base0 = nc.lookup_mloc(resv).addr
        off = [base0]
        def at(name, shape, dt, o=None):
            if o is None:
                o = off[0]
            itm = 2 if dt == BF16 else (1 if dt == U8 else 4)
            nbytes = int(np.prod(shape[1:])) * itm
            h = nc.alloc_sbuf_tensor_at(name, shape, dt, offset=o)
            off[0] = max(off[0], o + ((nbytes + 31) // 32) * 32)
            return h
        self.BIG = at("BIG", [128, 32768], BF16)
        self.WS = at("WS", [128, 8192], BF16)
        self.MUB = at("MUB", [128, S], F32)
        self.RSB = at("RSB", [128, S], F32)
        self.XIN = at("XIN", [128, 1024], F32)
        self.T1B = at("T1B", [128, 1024], F32)
        self.RES = at("RES", [128, 1024], F32)
        self.XNEW = at("XNEW", [128, 1024], F32)
        self.EVB = at("EVB", [128, 2048], BF16)
        self.SROW = at("SROW", [1, 2048], F32)
        self.EXPAT = at("EXPAT", [128, 2048], BF16)
        self.ATTNT = at("ATTNT", [128, 2048], BF16)
        self.IDN = at("IDNs", [128, 128], F32)
        self.IDNB = at("IDNBs", [128, 128], BF16)
        self.UTRI = at("UTRIs", [128, 128], F32)
        self.BIAST = at("BIASTs", [128, 128], F32)
        self.WINTB = at("WINTB", [128, 1024], BF16)
        self.ONE1F = at("ONE1Fs", [1, 128], F32)
        self.ONECB = at("ONECB", [128, 1], BF16)
        self.BQKV = at("BQKVs", [128, 24], F32)
        self.BOA = at("BOAs", [128, 32], F32)
        self.BBC = at("BBCs", [128, 32], F32)
        self.GN = at("GNs", [128, 1], F32)
        self.DTBB = at("DTBB", [128, 32], F32)
        self.ABB = at("ABB", [128, 32], F32)
        self.DTBR = at("DTBR", [1, 32], F32)
        self.ALBR = at("ALBR", [1, 32], F32)
        self.AROW = at("AROW", [1, 32], F32)
        self.TRS = at("TRS", [128, 768], BF16)
        self.MCOL = at("MCOL", [128, 16], F32)
        self.EPSB = at("EPSB", [128, 1], F32)
        self.DTQ = at("DTQ", [128, 192], F32)
        self.GGQ = at("GGQ", [128, 192], F32)
        self.GQ = at("GQ", [128, 192], F32)
        self.EGQ = at("EGQ", [128, 192], F32)
        self.EDQ = at("EDQ", [128, 192], F32)
        self.CDQ = at("CDQ", [128, 192], F32)
        self.GTB = at("GTB", [32, 768], F32)
        self.GROWH = at("GROWH", [1, 768], F32)
        self.SSH = at("SSH", [128, 32], F32)
        self.RSTDH = at("RSTDH", [128, 32], F32)
        self.RSROW = at("RSROW", [1, 768], F32)
        self.DECT = at("DECT", [128, 128], F32)
        self.YSB = at("YSB", [128, 128], F32)
        self.STB = at("STB", [128, 128], BF16)
        self.MU8 = at("MU8", [128, 1024], U8)
        self.RP2 = at("RP2", [128, 768], BF16)
        self.SDT = at("SDT", [128, 128], BF16)
        self.KDT = at("KDT", [128, 128], BF16)
        self.XCT = at("XCT", [128, 128], BF16)
        self.YB16 = at("YB16", [128, 128], BF16)
        self.RSTT = at("RSTT", [8, 128], F32)
        self.GLB = at("GLB", [128, 32], F32)
        self.TRS2 = at("TRS2", [128, 128], BF16)
        self.W01SB = at("W01SB", [128, 1280], BF16)
        self.MACC = at("MACC", [40, 1024], F32)
        self.STG32 = at("STG32", [128, 2048], F32)
        self.STGBF = at("STGBF", [128, 2048], BF16)
        base = off[0]
        self.STATES = at("STATES", [128, 4096], F32, o=base)
        self.ROPEC = at("ROPEC", [128, 768], F32, o=base + 16384)
        self.ROPES = at("ROPES", [128, 768], F32, o=base + 16384 + 3072)
        end_b = base + 16384 + 6144
        self.W01S = at("W01S", [128, 1280], F32, o=base)
        self.ENC01B = at("ENC01B", [40, DM], BF16, o=base + 5120)
        self.MBF = at("MBF", [40, DM], BF16, o=base + 13312)
        off[0] = max(end_b, base + 13312 + 8192)
        self.sbuf_used = off[0]
        assert off[0] <= base0 + RES_BYTES, f"arena overflow: {off[0]}"

    def _alloc_psum(self):
        nc = self.nc
        self.PKS = self.es.enter_context(nc.psum_tensor("PKS", [128, 2048], F32))
        self.PK4H = self.es.enter_context(nc.psum_tensor("PK4", [128, 512], F32))
        self.PK5H = self.es.enter_context(nc.psum_tensor("PK5", [128, 512], F32))
        self.PTA = self.es.enter_context(nc.psum_tensor("PTA", [128, 1024], BF16))
        self.PTB = self.es.enter_context(nc.psum_tensor("PTB", [128, 1024], BF16))

    def pk(self, i, rows=None):
        if i < 4:
            return self.PKS[:, i * 512:(i + 1) * 512] if rows is None \
                else self.PKS[rows[0]:rows[1], i * 512:(i + 1) * 512]
        h = self.PK4H if i == 4 else self.PK5H
        return h[:, :] if rows is None else h[rows[0]:rows[1], :]

    def _regions(self):
        R = lambda n, m=False: Reg(n, m)
        self.rPK = [R(f"pk{i}") for i in range(6)]
        self.rPT = [R("pta"), R("ptb")]
        self.rXIN = [R("xin0"), R("xin1")]
        self.rT1 = R("t1")
        self.rWS = [R("ws0"), R("ws1")]
        self.rEVB = [R(f"evb{i}") for i in range(4)]
        self.rRES = [R("res0"), R("res1")]
        self.rXNEW = [R("xn0"), R("xn1")]
        self.rSTG = [R("stg0"), R("stg1")]
        self.rSTGB = [R("stgb0"), R("stgb1")]
        self.rMUB = R("mub", True)
        self.rRSB = R("rsb", True)
        self.rSROW = [R(f"srow{i}") for i in range(4)]
        self.rMCOL = [R(f"mcol{i}") for i in range(16)]
        self.rEXP = [R(f"exp{i}") for i in range(4)]
        self.rATT = [R(f"att{i}") for i in range(16)]
        self.rTRS = [R("trs0"), R("trs1"), R("trs2")]
        self.rCONST = R("const", True)
        self.rBIGA = R("bigA", True)
        self.rBIGB = R("bigB", True)
        self.rBIGC = R("bigC", True)
        self.big_live = [self.rBIGA, self.rBIGB, self.rBIGC]
        self.rXT = {0: R("xt0"), 1: R("xt1", True), 2: R("xt2", True), 3: R("xt3", True),
                    4: R("yt", True)}
        self.rXTg = {k: [R(f"xt{k}g{i}", True) for i in range(4)] for i2 in [0] for k in (1, 2, 3, 4)}
        self.rWB = {k: R(f"wb_{k}", True) for k in
                    ("Wqkv", "Woa", "Weff", "Wq", "Wk", "Wv", "Wg", "Wo")}
        self.rQKVa = R("qkva", True)
        self.rAVTa = R("avta", True)
        self.rYTb = R("ytb", True)

    def ph_consts(self):
        d = self
        for dst, src in [(d.IDN, d.IDN_d), (d.UTRI, d.UTRI_d), (d.BIAST, d.BIAST_d),
                         (d.ONE1F, d.ONE1F_d), (d.BQKV, d.bqkv_c), (d.BOA, d.bo_c),
                         (d.BBC, d.bb_c), (d.GN, d.gn_c), (d.DTBR, d.dtb_r),
                         (d.ALBR, d.alb_r)]:
            self.DMA("sync", dst[:], src, "cl", writes=[d.rCONST])
        rAR = Reg("arow")
        self.E("vector", lambda v: v.tensor_copy(d.IDNB[:], d.IDN[:]),
               reads=[d.rCONST], writes=[d.rCONST])
        self.E("vector", lambda v: v.memset(d.ONECB[:], 1.0), writes=[d.rCONST])
        self.E("vector", lambda v: v.memset(d.EPSB[:], LNEPS), writes=[d.rCONST])
        self.E("scalar", lambda s: s.activation(d.AROW[:], d.ALBR[:], AF.Exp),
               reads=[d.rCONST], writes=[rAR])
        self.E("vector", lambda v: v.tensor_scalar(d.AROW[:], d.AROW[:], -1.0, None, ALU.mult),
               reads=[rAR], writes=[rAR])
        self.E("tensor", lambda t: t.matmul(self.pk(5)[:, 0:32], d.ONE1F[:], d.DTBR[:],
                                            start=True, stop=True),
               reads=[d.rCONST], writes=[self.rPK[5]])
        self.E("tensor", lambda t: t.matmul(self.pk(5)[:, 32:64], d.ONE1F[:], d.AROW[:],
                                            start=True, stop=True),
               reads=[d.rCONST, rAR], writes=[self.rPK[5]])
        self.E("vector", lambda v: v.tensor_copy(d.DTBB[:], self.pk(5)[:, 0:32]),
               reads=[self.rPK[5]], writes=[d.rCONST])
        self.E("vector", lambda v: v.tensor_copy(d.ABB[:], self.pk(5)[:, 32:64]),
               reads=[self.rPK[5]], writes=[d.rCONST])

    def ph_cast(self, srcL, dstB, nmb, fin, jobid, wreg):
        d = self
        nh = fin // 1024
        if not hasattr(self, "_cast_ctr"):
            self._cast_ctr = 0
        for mb in range(nmb):
            for h in range(nh):
                buf = self._cast_ctr % 2
                self._cast_ctr += 1
                sl32 = d.STG32[:, buf * 1024:(buf + 1) * 1024]
                slbf = d.STGBF[:, buf * 1024:(buf + 1) * 1024]
                self.DMA("gpsimd", sl32, srcL[mb, :, h * 1024:(h + 1) * 1024],
                         f"c{jobid}l{buf}", writes=[d.rSTG[buf]])
                self.E("gpsimd", lambda g, a=slbf, b=sl32: g.tensor_copy(a, b),
                       reads=[d.rSTG[buf]], writes=[d.rSTGB[buf]])
                self.DMA("gpsimd", dstB[mb, :, h * 1024:(h + 1) * 1024], slbf,
                         f"c{jobid}s{buf}", reads=[d.rSTGB[buf]], writes=[wreg])
                yield

    def ph_stats(self, xk, xsrc):
        d = self
        self.rMUB.turnover()
        self.rRSB.turnover()
        for tg in range(4):
            for c in range(CF):
                b = c % 2
                xs = d.XIN[:, b * 512:(b + 1) * 512]
                xb = d.EVB[:, b * 512:(b + 1) * 512]
                sq = d.EVB[:, 1024 + b * 512:1024 + (b + 1) * 512]
                self.DMA("sync", xs, xsrc[c * 128:(c + 1) * 128, tg * 512:(tg + 1) * 512],
                         f"xi{b}", reads=[self.xreg(xk, tg)], writes=[d.rXIN[b]])
                self.E("vector", lambda v, a=xb, s_=xs: v.tensor_copy(a, s_),
                       reads=[d.rXIN[b]], writes=[d.rEVB[b]])
                self.E("scalar", lambda s, a=sq, b_=xb: s.activation(a, b_, AF.Square),
                       reads=[d.rEVB[b]], writes=[d.rEVB[2 + b]])
                self.E("tensor", lambda t, a=xb, st=(c == 0), z=(c == CF - 1): t.matmul(
                    self.pk(4, (0, 1)), d.ONECB[:], a, start=st, stop=z),
                    reads=[d.rCONST, d.rEVB[b]],
                    writes=[self.rPK[4]])
                self.E("tensor", lambda t, a=sq, st=(c == 0), z=(c == CF - 1): t.matmul(
                    self.pk(5, (0, 1)), d.ONECB[:], a, start=st, stop=z),
                    reads=[d.rCONST, d.rEVB[2 + b]],
                    writes=[self.rPK[5]])
                yield
            self._stats_rows(tg * 512, 512)

    def _stats_rows(self, tok0, wdt, sm=4, sq=5):
        d = self
        mu_r = d.SROW[0:1, 0:wdt]; sq_r = d.SROW[0:1, 512:512 + wdt]
        var_r = d.SROW[0:1, 1024:1024 + wdt]
        rM, rQ, rV = d.rSROW[0], d.rSROW[1], d.rSROW[2]
        pks = lambda i, r=None: self.pk(i, r)
        self.E("vector", lambda v: v.tensor_scalar(mu_r, self.pk(sm, (0, 1))[:, 0:wdt],
                                                   1.0 / DM, None, ALU.mult),
               reads=[self.rPK[sm]], writes=[rM])
        self.E("vector", lambda v: v.tensor_scalar(sq_r, self.pk(sq, (0, 1))[:, 0:wdt],
                                                   1.0 / DM, None, ALU.mult),
               reads=[self.rPK[sq]], writes=[rQ])
        self.E("vector", lambda v: v.tensor_mul(var_r, mu_r, mu_r), reads=[rM], writes=[rV])
        self.E("vector", lambda v: v.tensor_sub(var_r, sq_r, var_r), reads=[rQ, rV], writes=[rV])
        self.E("scalar", lambda s: s.activation(var_r, var_r, AF.Ln, bias=d.EPSB[0:1, :]),
               reads=[rV, d.rCONST], writes=[rV])
        self.E("scalar", lambda s: s.activation(var_r, var_r, AF.Exp, scale=-0.5),
               reads=[rV], writes=[rV])
        self.E("tensor", lambda t: t.matmul(self.pk(sm)[:, 0:wdt], d.ONE1F[:], mu_r,
                                            start=True, stop=True),
               reads=[d.rCONST, rM], writes=[self.rPK[sm]])
        self.E("tensor", lambda t: t.matmul(self.pk(sq)[:, 0:wdt], d.ONE1F[:], var_r,
                                            start=True, stop=True),
               reads=[d.rCONST, rV], writes=[self.rPK[sq]])
        self.E("vector", lambda v: v.tensor_copy(d.MUB[:, tok0:tok0 + wdt],
                                                 self.pk(sm)[:, 0:wdt]),
               reads=[self.rPK[sm]], writes=[d.rMUB])
        self.E("vector", lambda v: v.tensor_copy(d.RSB[:, tok0:tok0 + wdt],
                                                 self.pk(sq)[:, 0:wdt]),
               reads=[self.rPK[sq]], writes=[d.rRSB])

    def xreg(self, k, g):
        return self.rXT[0] if k == 0 else self.rXTg[k][g]

    def bigseed(self, *regs):
        base = merge(*[r.w for r in regs], *[r.r for r in regs])
        for r in regs:
            r.w = []
            r.r = base

    def bignew(self, n):
        base = merge(*[r.w for r in self.big_live], *[r.r for r in self.big_live])
        out = []
        for i in range(n):
            r = Reg(f"big{i}", multi=True)
            r.r = list(base)
            out.append(r)
        self.big_live = out
        return out

    def ph_xm(self, xk, xsrc, t0, tw, big_off, bigreg):
        d = self
        assert tw <= 1024
        ctr = 0
        for c in range(CF):
            for s0 in range(0, tw, 512):
                w = min(512, tw - s0)
                b = ctr % 2
                ctr += 1
                tt = t0 + s0
                xreads = [self.xreg(xk, g) for g in range(tt // 512, (tt + w - 1) // 512 + 1)]
                xs = d.XIN[:, b * 512:b * 512 + w]
                dst = d.BIG[:, big_off + c * tw + s0: big_off + c * tw + s0 + w]
                self.DMA("sync", xs, xsrc[c * 128:(c + 1) * 128, tt:tt + w],
                         f"xi{b}", reads=xreads, writes=[d.rXIN[b]])
                self.E("vector", lambda v, s_=xs, w=w, tt=tt: v.tensor_tensor(
                    d.T1B[:, 0:w], s_, d.MUB[:, tt:tt + w], ALU.subtract),
                    reads=[d.rXIN[b], d.rMUB], writes=[d.rT1])
                self.E("vector", lambda v, o=dst, w=w, tt=tt: v.tensor_tensor(
                    o, d.T1B[:, 0:w], d.RSB[:, tt:tt + w], ALU.mult),
                    reads=[d.rT1, d.rRSB], writes=[bigreg])
            yield

    def ph_attn_a(self):
        d = self
        yield from self.ph_stats(0, self.xT0)
        for h in range(2):
            self.rBIGA.turnover()
            yield from self.ph_xm(0, self.xT0, h * 1024, 1024, 0, self.rBIGA)
            for mb in range(24):
                wb = mb % 2
                ws = d.WS[:, wb * 4096:(wb + 1) * 4096]
                self.DMA("sync", ws, self.WqkvB[mb], f"ws{wb}",
                         reads=[self.rWB["Wqkv"]], writes=[d.rWS[wb]])
                pki = [(mb % 2) * 2, (mb % 2) * 2 + 1]
                def emit(t, ws=ws, i0=pki[0], i1=pki[1]):
                    last = None
                    for c in range(CF):
                        t.matmul(self.pk(i0), ws[:, c * 128:(c + 1) * 128],
                                 d.BIG[:, c * 1024: c * 1024 + 512],
                                 start=(c == 0), stop=(c == CF - 1))
                        last = t.matmul(self.pk(i1), ws[:, c * 128:(c + 1) * 128],
                                        d.BIG[:, c * 1024 + 512: c * 1024 + 1024],
                                        start=(c == 0), stop=(c == CF - 1))
                    return last
                self.E("tensor", emit, reads=[d.rWS[wb], self.rBIGA],
                       writes=[self.rPK[pki[0]], self.rPK[pki[1]]])
                for ng in range(2):
                    eb = mb % 2
                    ev = d.EVB[:, eb * 1024 + ng * 512: eb * 1024 + (ng + 1) * 512]
                    evr = d.rEVB[eb * 2 + ng]
                    self.E("vector", lambda v, o=ev, p=self.pk(pki[ng]), m=mb: v.tensor_scalar(
                        o, p, d.BQKV[:, m:m + 1], None, ALU.add),
                        reads=[self.rPK[pki[ng]], d.rCONST], writes=[evr])
                    if mb < 16:
                        dst = (self.qTa[mb] if mb < 8 else self.kTa[mb - 8])
                        self.DMA("scalar", dst[:, h * 1024 + ng * 512: h * 1024 + (ng + 1) * 512],
                                 ev, f"ev{eb * 2 + ng}", reads=[evr], writes=[self.rQKVa])
                    else:
                        for tt in range(4):
                            pb = tt % 2
                            pt = self.PTA if pb == 0 else self.PTB
                            self.E("tensor", lambda t, p=pt, i=ev, tt=tt: t.transpose(
                                p[:, 0:128], i[:, tt * 128:(tt + 1) * 128], d.IDNB[:]),
                                reads=[evr, d.rCONST], writes=[self.rPT[pb]])
                            ev2 = d.TRS[:, pb * 128:(pb + 1) * 128]
                            self.E("vector", lambda v, o=ev2, p=pt: v.tensor_copy(o, p[:, 0:128]),
                                   reads=[self.rPT[pb]], writes=[d.rTRS[pb]])
                            tc = h * 8 + ng * 4 + tt
                            self.DMA("scalar", self.vta[tc, :, (mb - 16) * 128:(mb - 15) * 128],
                                     ev2, f"vt{pb}", reads=[d.rTRS[pb]], writes=[self.rQKVa])
                yield
        self.bigseed(self.rBIGA, self.rBIGB)
        for dc in range(8):
            self.DMA("sync", d.BIG[:, dc * 2048:(dc + 1) * 2048], self.kTa[dc], "akv",
                     reads=[self.rQKVa], writes=[self.rBIGA])
        for tc in range(16):
            self.DMA("sync", d.BIG[:, 16384 + tc * 1024:16384 + (tc + 1) * 1024],
                     self.vta[tc], "akv", reads=[self.rQKVa], writes=[self.rBIGB])
        tot = [("akv", self.w.now("akv"))]
        self.rBIGA.w = merge(self.rBIGA.w, tot)
        self.rBIGB.w = merge(self.rBIGB.w, tot)
        for qt in range(16):
            qb = qt % 2
            qtl = d.EVB[:, qb * 1024:(qb + 1) * 1024]
            qreg = [d.rEVB[qb * 2], d.rEVB[qb * 2 + 1]]
            for dc in range(8):
                self.DMA("sync", qtl[:, dc * 128:(dc + 1) * 128],
                         self.qTa[dc, :, qt * 128:(qt + 1) * 128], f"qt{qb}{dc % 2}",
                         reads=[self.rQKVa], writes=[qreg[dc % 2]])
            for ng in range(4):
                def emit(t, ng=ng, qtl=qtl):
                    last = None
                    for dc in range(8):
                        last = t.matmul(self.pk(ng), qtl[:, dc * 128:(dc + 1) * 128],
                                        d.BIG[:, dc * 2048 + ng * 512: dc * 2048 + (ng + 1) * 512],
                                        start=(dc == 0), stop=(dc == 7))
                    return last
                self.E("tensor", emit, reads=qreg + [self.rBIGA], writes=[self.rPK[ng]])
            self.E("vector", lambda v: v.reduce_max(d.MCOL[:, 4:5], self.PKS[:, :], axis=AX.X),
                   reads=[self.rPK[0], self.rPK[1], self.rPK[2], self.rPK[3]],
                   writes=[d.rMCOL[4]])
            self.E("scalar", lambda s: s.activation(d.MCOL[:, 6:7], d.MCOL[:, 4:5], AF.Copy,
                                                    scale=-1.0 / 32.0),
                   reads=[d.rMCOL[4]], writes=[d.rMCOL[6]])
            for ng in range(4):
                self.E("scalar", lambda s, ng=ng: s.activation(
                    d.EXPAT[:, ng * 512:(ng + 1) * 512], self.pk(ng), AF.Exp,
                    bias=d.MCOL[:, 6:7], scale=1.0 / 32.0,
                    accum_out=d.MCOL[:, 8 + ng:9 + ng]),
                    reads=[self.rPK[ng], d.rMCOL[6]],
                    writes=[d.rEXP[ng], d.rMCOL[8 + ng]])
            self.E("gpsimd", lambda g: g.tensor_add(d.MCOL[:, 12:13], d.MCOL[:, 8:9], d.MCOL[:, 9:10]),
                   reads=[d.rMCOL[8], d.rMCOL[9]], writes=[d.rMCOL[12]])
            self.E("gpsimd", lambda g: g.tensor_add(d.MCOL[:, 13:14], d.MCOL[:, 10:11], d.MCOL[:, 11:12]),
                   reads=[d.rMCOL[10], d.rMCOL[11]], writes=[d.rMCOL[13]])
            self.E("gpsimd", lambda g: g.tensor_add(d.MCOL[:, 12:13], d.MCOL[:, 12:13], d.MCOL[:, 13:14]),
                   reads=[d.rMCOL[13]], writes=[d.rMCOL[12]])
            self.E("scalar", lambda s: s.activation(d.MCOL[:, 14:15], d.MCOL[:, 12:13], AF.Ln),
                   reads=[d.rMCOL[12]], writes=[d.rMCOL[14]])
            self.E("scalar", lambda s: s.activation(d.MCOL[:, 14:15], d.MCOL[:, 14:15], AF.Exp,
                                                    scale=-1.0),
                   reads=[d.rMCOL[14]], writes=[d.rMCOL[14]])
            for ck in range(16):
                pb = ck % 2
                pt = self.PTA if pb == 0 else self.PTB
                self.E("tensor", lambda t, p=pt, ck=ck: t.transpose(
                    p[:, 0:128], d.EXPAT[:, ck * 128:(ck + 1) * 128], d.IDNB[:]),
                    reads=[d.rEXP[ck // 4], d.rCONST], writes=[self.rPT[pb]])
                self.E("vector", lambda v, p=pt, ck=ck: v.tensor_copy(
                    d.ATTNT[:, ck * 128:(ck + 1) * 128], p[:, 0:128]),
                    reads=[self.rPT[pb]], writes=[d.rATT[ck]])
            for ngd in range(2):
                def emit(t, ngd=ngd):
                    last = None
                    for ck in range(16):
                        last = t.matmul(self.pk(4 + ngd), d.ATTNT[:, ck * 128:(ck + 1) * 128],
                                        d.BIG[:, 16384 + ck * 1024 + ngd * 512:
                                              16384 + ck * 1024 + (ngd + 1) * 512],
                                        start=(ck == 0), stop=(ck == 15))
                    return last
                self.E("tensor", emit, reads=d.rATT + [self.rBIGB], writes=[self.rPK[4 + ngd]])
            for ngd in range(2):
                self.E("vector", lambda v, ngd=ngd: v.tensor_scalar(
                    d.ATTNT[:, ngd * 512:(ngd + 1) * 512], self.pk(4 + ngd),
                    d.MCOL[:, 14:15], None, ALU.mult),
                    reads=[self.rPK[4 + ngd], d.rMCOL[14]],
                    writes=[d.rATT[ngd * 4 + k] for k in range(4)])
            for dc in range(8):
                pb = dc % 2
                pt = self.PTA if pb == 0 else self.PTB
                self.E("tensor", lambda t, p=pt, dc=dc: t.transpose(
                    p[:, 0:128], d.ATTNT[:, dc * 128:(dc + 1) * 128], d.IDNB[:]),
                    reads=[d.rATT[dc], d.rCONST], writes=[self.rPT[pb]])
                self.E("vector", lambda v, p=pt, dc=dc: v.tensor_copy(
                    d.ATTNT[:, 1024 + dc * 128:1024 + (dc + 1) * 128], p[:, 0:128]),
                    reads=[self.rPT[pb]], writes=[d.rATT[8 + dc]])
                self.DMA("scalar", self.avTa[dc, :, qt * 128:(qt + 1) * 128],
                         d.ATTNT[:, 1024 + dc * 128:1024 + (dc + 1) * 128], f"avs{dc}",
                         reads=[d.rATT[8 + dc]], writes=[self.rAVTa])
            yield

    def ph_attn_a_out(self):
        d = self
        self.rMUB.turnover()
        self.rRSB.turnover()
        self.bigseed(self.rBIGC, self.rBIGA, self.rBIGB)
        for og in range(4):
            self.rBIGC.turnover()
            for dc in range(8):
                self.DMA("sync", d.BIG[:, dc * 512:(dc + 1) * 512],
                         self.avTa[dc, :, og * 512:(og + 1) * 512], "sl",
                         reads=[self.rAVTa], writes=[self.rBIGC])
            for mb in range(32):
                wb = mb % 2
                ws = d.WS[:, wb * 4096: wb * 4096 + 1024]
                self.DMA("sync", ws, self.WoaB[mb], f"ws{wb}",
                         reads=[self.rWB["Woa"]], writes=[d.rWS[wb]])
                pki = mb % 2
                rb = mb % 2
                res = d.RES[:, rb * 512:(rb + 1) * 512]
                self.DMA("sync", res, self.xT0[mb * 128:(mb + 1) * 128, og * 512:(og + 1) * 512],
                         f"xr{rb}", reads=[self.rXT[0]], writes=[d.rRES[rb]])
                def emit(t, ws=ws, i=pki):
                    last = None
                    for dc in range(8):
                        last = t.matmul(self.pk(i), ws[:, dc * 128:(dc + 1) * 128],
                                        d.BIG[:, dc * 512:(dc + 1) * 512],
                                        start=(dc == 0), stop=(dc == 7))
                    return last
                self.E("tensor", emit, reads=[d.rWS[wb], self.rBIGC], writes=[self.rPK[pki]])
                xb = mb % 2
                xn = d.XNEW[:, xb * 512:(xb + 1) * 512]
                self.E("vector", lambda v, o=xn, p=self.pk(pki), m=mb, r=res: v.scalar_tensor_tensor(
                    o, p, d.BOA[:, m:m + 1], r, ALU.add, ALU.add),
                    reads=[self.rPK[pki], d.rCONST, d.rRES[rb]], writes=[d.rXNEW[xb]])
                self.DMA("scalar", self.xT1[mb * 128:(mb + 1) * 128, og * 512:(og + 1) * 512],
                         xn, f"xw{xb}", reads=[d.rXNEW[xb]], writes=[self.rXTg[1][og]])
                eb = mb % 2
                xbf = d.EVB[:, eb * 512:(eb + 1) * 512]
                sqb = d.EVB[:, 1024 + eb * 512:1024 + (eb + 1) * 512]
                self.E("vector", lambda v, a=xbf, i=xn: v.tensor_copy(a, i),
                       reads=[d.rXNEW[xb]], writes=[d.rEVB[eb]])
                self.E("scalar", lambda s, a=sqb, i=xbf: s.activation(a, i, AF.Square),
                       reads=[d.rEVB[eb]], writes=[d.rEVB[2 + eb]])
                self.E("tensor", lambda t, a=xbf, st=(mb == 0), z=(mb == 31): t.matmul(
                    self.pk(4, (0, 1)), d.ONECB[:], a, start=st, stop=z),
                    reads=[d.rCONST, d.rEVB[eb]], writes=[self.rPK[4]])
                self.E("tensor", lambda t, a=sqb, st=(mb == 0), z=(mb == 31): t.matmul(
                    self.pk(5, (0, 1)), d.ONECB[:], a, start=st, stop=z),
                    reads=[d.rCONST, d.rEVB[2 + eb]], writes=[self.rPK[5]])
                yield
            self._stats_rows(og * 512, 512)

    def finish(self):
        allw = [(s, self.w.cnt[s]) for s in list(self.w.cnt)]
        self.w.op("sync", lambda e: e.nop(), wait=merge(allw))

    def ph_prep(self):
        d = self
        rW01 = Reg("w01s")
        rCst = Reg("cstg"); rM8 = Reg("m8"); rMskf = Reg("mskf")
        w01v = d.W01S[:, 0:1280].rearrange("p (c r) -> p c r", r=40)
        for part, (cl, ml, R, roff) in enumerate(
                [(d.c0L, d.m0L, 32, 0), (d.c1L, d.m1L, 8, 32)]):
            for e in range(3):
                n = 32 * R
                self.DMA("sync", d.RES[:, 0:n], cl[e], "cof", writes=[d.rRES[0], d.rRES[1]])
                self.DMA("sync", d.MU8[:, 0:n], ml[e], "msk", writes=[rM8])
                self.E("vector", lambda v, n=n: v.tensor_copy(d.XNEW[:, 0:n], d.MU8[:, 0:n]),
                       reads=[rM8], writes=[rMskf])
                self.E("vector", lambda v, n=n: v.tensor_mul(
                    d.XNEW[:, 0:n], d.RES[:, 0:n], d.XNEW[:, 0:n]),
                    reads=[d.rRES[0], d.rRES[1], rMskf], writes=[rMskf])
                src = d.XNEW[:, 0:n].rearrange("p (c r) -> p c r", r=R)
                dstv = w01v[:, :, roff:roff + R]
                if e == 0:
                    self.E("vector", lambda v, a=dstv, b=src: v.tensor_copy(a, b),
                           reads=[rMskf], writes=[rW01])
                else:
                    self.E("vector", lambda v, a=dstv, b=src: v.tensor_tensor(a, b, a, ALU.add),
                           reads=[rMskf, rW01], writes=[rW01])
                yield
        rENC = Reg("enc01")
        for hh in range(4):
            self.DMA("sync", d.RES[0:40, 0:1024], d.enc01_d[:, hh * 1024:(hh + 1) * 1024],
                     "enc", writes=[d.rRES[0], d.rRES[1]])
            self.E("vector", lambda v, hh=hh: v.tensor_copy(
                d.ENC01B[:, hh * 1024:(hh + 1) * 1024], d.RES[0:40, 0:1024]),
                reads=[d.rRES[0], d.rRES[1]], writes=[rENC])
            yield
        rWIN = Reg("wint")
        for c in range(CF):
            b = c % 2
            self.DMA("sync", d.RES[:, b * 512:b * 512 + 32],
                     d.WinT_d[c * 128:(c + 1) * 128, :], f"win{b}", writes=[d.rRES[b]])
            self.E("vector", lambda v, c=c, b=b: v.tensor_copy(
                d.WINTB[:, c * 32:(c + 1) * 32], d.RES[:, b * 512:b * 512 + 32]),
                reads=[d.rRES[b]], writes=[d.rCONST])
            yield
        rW01B = Reg("w01sb")
        self.E("vector", lambda v: v.tensor_copy(d.W01SB[:], d.W01S[:]),
               reads=[rW01], writes=[rW01B])
        rMBF = Reg("mbf")
        self.rMBF = rMBF
        rMAC = Reg("macc")
        self.E("vector", lambda v: v.memset(d.MACC[:], 0.0), writes=[rMAC])
        for n2 in range(4):
            for cc in range(2 * CF):
                c = cc % CF
                src = d.KM if cc < CF else d.KML
                buf = cc % 2
                sl32 = d.T1B[:, 0:1024] if False else d.XNEW[:, 0:1024]
                slbf = d.EXPAT[:, buf * 1024:(buf + 1) * 1024]
                self.DMA("sync", sl32, src[c * 128:(c + 1) * 128, n2 * 1024:(n2 + 1) * 1024],
                         f"km{buf}", writes=[d.rXNEW[0], d.rXNEW[1]])
                self.E("vector", lambda v, a=slbf, b=sl32: v.tensor_copy(a, b),
                       reads=[d.rXNEW[0], d.rXNEW[1]],
                       writes=[d.rEXP[buf * 2], d.rEXP[buf * 2 + 1]])
                def emitm(t, slbf=slbf, c=c):
                    t.matmul(self.PKS[0:40, 0:512],
                             d.W01SB[:, c * 40:(c + 1) * 40], slbf[:, 0:512],
                             start=True, stop=True)
                    return t.matmul(self.PKS[0:40, 512:1024],
                                    d.W01SB[:, c * 40:(c + 1) * 40], slbf[:, 512:1024],
                                    start=True, stop=True)
                self.E("tensor", emitm, reads=[rW01B, d.rEXP[buf * 2], d.rEXP[buf * 2 + 1]],
                       writes=[self.rPK[0], self.rPK[1]])
                self.E("vector", lambda v: v.tensor_add(
                    d.MACC[0:40, :], d.MACC[0:40, :], self.PKS[0:40, 0:1024]),
                    reads=[self.rPK[0], self.rPK[1], rMAC], writes=[rMAC])
                yield
            self.E("vector", lambda v, n2=n2: v.tensor_copy(
                d.MBF[:, n2 * 1024:(n2 + 1) * 1024], d.MACC[0:40, 0:1024]),
                reads=[rMAC], writes=[rMBF])
            self.E("vector", lambda v: v.memset(d.MACC[:], 0.0), writes=[rMAC])
            yield
        for fb in range(CF):
            for g in range(8):
                sl = g % 2
                res = d.RES[:, sl * 512:(sl + 1) * 512]
                self.DMA("sync", res, d.WbT_d[fb * 128:(fb + 1) * 128, g * 512:(g + 1) * 512],
                         f"wbt{sl}", writes=[d.rRES[sl]])
                pki = g % 2
                self.E("tensor", lambda t, fb=fb, g=g, i=pki: t.matmul(
                    self.pk(i), d.ENC01B[:, fb * 128:(fb + 1) * 128],
                    d.MBF[:, g * 512:(g + 1) * 512], start=True, stop=True),
                    reads=[rENC, rMBF], writes=[self.rPK[pki]])
                eb = g % 2
                ev = d.EVB[:, eb * 512:(eb + 1) * 512]
                self.E("vector", lambda v, o=ev, p=self.pk(pki), r=res: v.scalar_tensor_tensor(
                    o, p, 0.5, r, ALU.mult, ALU.add),
                    reads=[self.rPK[pki], d.rRES[sl]], writes=[d.rEVB[eb]])
                for q in range(4):
                    self.DMA("scalar", self.WeffB[g * 4 + q, :, fb * 128:(fb + 1) * 128],
                             ev[:, q * 128:(q + 1) * 128], f"wf{eb}",
                             reads=[d.rEVB[eb]], writes=[self.rWB["Weff"]])
                yield

    def ph_ffn(self, xsrc, ksrc, xdst, kdst, last):
        d = self
        stat_pk = {0: (2, 4), 1: (3, 5)}
        for h in range(2):
            rXM = self.bignew(1)[0]
            yield from self.ph_xm(ksrc, xsrc, h * 1024, 1024, 0, rXM)
            for mb in range(32):
                wb = mb % 2
                ws = d.WS[:, wb * 4096:(wb + 1) * 4096]
                self.DMA("sync", ws, self.WeffB[mb], f"ws{wb}",
                         reads=[self.rWB["Weff"]], writes=[d.rWS[wb]])
                for ngg in range(2):
                    def emit(t, ws=ws, ngg=ngg):
                        last_i = None
                        for c in range(CF):
                            last_i = t.matmul(self.pk(ngg), ws[:, c * 128:(c + 1) * 128],
                                              d.BIG[:, c * 1024 + ngg * 512: c * 1024 + ngg * 512 + 512],
                                              start=(c == 0), stop=(c == CF - 1))
                        return last_i
                    self.E("tensor", emit, reads=[d.rWS[wb], rXM],
                           writes=[self.rPK[ngg]])
                for ng in range(2):
                    rb = ng
                    res = d.RES[:, rb * 512:(rb + 1) * 512]
                    self.DMA("sync", res,
                             xsrc[mb * 128:(mb + 1) * 128, h * 1024 + ng * 512: h * 1024 + (ng + 1) * 512],
                             f"xr{rb}", reads=[self.xreg(ksrc, h * 2 + ng)], writes=[d.rRES[rb]])
                    xb = ng
                    xn = d.XNEW[:, xb * 512:(xb + 1) * 512]
                    self.E("vector", lambda v, o=xn, p=self.pk(ng), m=mb, r=res: v.scalar_tensor_tensor(
                        o, p, d.BBC[:, m:m + 1], r, ALU.add, ALU.add),
                        reads=[self.rPK[ng], d.rCONST, d.rRES[rb]], writes=[d.rXNEW[xb]])
                    self.DMA("scalar",
                             xdst[mb * 128:(mb + 1) * 128, h * 1024 + ng * 512: h * 1024 + (ng + 1) * 512],
                             xn, f"xw{xb}", reads=[d.rXNEW[xb]], writes=[self.xreg(kdst, h * 2 + ng)])
                    if not last:
                        eb = ng
                        xbf = d.EVB[:, eb * 512:(eb + 1) * 512]
                        sqb = d.EVB[:, 1024 + eb * 512:1024 + (eb + 1) * 512]
                        self.E("vector", lambda v, a=xbf, i=xn: v.tensor_copy(a, i),
                               reads=[d.rXNEW[xb]], writes=[d.rEVB[eb]])
                        self.E("scalar", lambda sc, a=sqb, i=xbf: sc.activation(a, i, AF.Square),
                               reads=[d.rEVB[eb]], writes=[d.rEVB[2 + eb]])
                        smb, sqk = stat_pk[ng]
                        self.E("tensor", lambda t, a=xbf, st=(mb == 0), z=(mb == 31), i=smb: t.matmul(
                            self.pk(i, (0, 1)), d.ONECB[:], a, start=st, stop=z),
                            reads=[d.rCONST, d.rEVB[eb]], writes=[self.rPK[smb]])
                        self.E("tensor", lambda t, a=sqb, st=(mb == 0), z=(mb == 31), i=sqk: t.matmul(
                            self.pk(i, (0, 1)), d.ONECB[:], a, start=st, stop=z),
                            reads=[d.rCONST, d.rEVB[2 + eb]], writes=[self.rPK[sqk]])
                yield
            if not last:
                if h == 0:
                    self.rMUB.turnover()
                    self.rRSB.turnover()
                for ng in range(2):
                    smb, sqk = stat_pk[ng]
                    self._stats_rows(h * 1024 + ng * 512, 512, smb, sqk)
                yield

    def ph_attn_b(self):
        d = self
        KOFF = 16384
        SL = lambda i: d.BIG[:, KOFF + i * 512: KOFF + (i + 1) * 512]
        rST = Reg("states")
        rROPE = Reg("rope")
        rDT = Reg("dtq"); rGG = Reg("ggq"); rG = Reg("gq"); rEG = Reg("egq")
        rED = Reg("edq"); rCD = Reg("cdq"); rGTB = Reg("gtb"); rGRW = Reg("growh")
        rSSH = [Reg(f"ssh{i}") for i in range(8)]; rRSH = [Reg(f"rsh{i}") for i in range(8)]
        rRST = Reg("rstt"); rRSR = Reg("rsrow")
        rDEC = Reg("dect"); rYS = Reg("ysb"); rSTB = Reg("stb"); rSD = Reg("sdt")
        rKD = Reg("kdt"); rXC = Reg("xct"); rYB = Reg("yb16"); rGLB = Reg("glb")
        rRP1 = d.rTRS[0]; rRP2 = Reg("rp2")
        self.E("vector", lambda v: v.memset(d.STATES[:], 0.0), writes=[rST])
        base = merge(*[r.w for r in self.big_live], *[r.r for r in self.big_live])
        rXMB = Reg("xmb", multi=True); rXMB.r = list(base)
        tr = {}
        for nm, cnt in (("kt", 2), ("vt", 2), ("ktok", 2), ("vtok", 2), ("qt", 2), ("yth", 8)):
            tr[nm] = [Reg(f"{nm}{i}", multi=True) for i in range(cnt)]
            for rg in tr[nm]:
                rg.r = list(base)
        self.big_live = [rXMB] + [rg for nm in tr for rg in tr[nm]]
        kt = lambda p: SL(0 + p); vt = lambda p: SL(2 + p)
        ktok = lambda p: SL(4 + p); vtok = lambda p: SL(6 + p)
        qts = lambda p: SL(8 + p); yth = lambda p: SL(10 + p)

        def gen_A(bi, h, xm):
            kvh = h // 4
            kp = kvh % 2
            hp = h % 2
            if h % 4 == 0:
                for (dstf, dreg, WB, wname, wb) in ((kt, tr["kt"][kp], self.WkB, "Wk", 0),
                                                    (vt, tr["vt"][kp], self.WvB, "Wv", 1)):
                    ws = d.WS[:, wb * 4096:(wb + 1) * 4096]
                    self.DMA("sync", ws, WB[kvh], f"ws{wb}",
                             reads=[self.rWB[wname]], writes=[d.rWS[wb]])
                    def emitp(t, ws=ws, xm=xm):
                        last_i = None
                        for c in range(CF):
                            last_i = t.matmul(self.pk(0), ws[:, c * 128:(c + 1) * 128],
                                              xm(c, 0, 512), start=(c == 0), stop=(c == CF - 1))
                        return last_i
                    self.E("tensor", emitp, reads=[d.rWS[wb], rXMB], writes=[self.rPK[0]])
                    self.E("vector", lambda v, dstf=dstf, kp=kp: v.tensor_copy(
                        dstf(kp), self.pk(0)), reads=[self.rPK[0]], writes=[dreg])
                    yield
                self._rope(kt(kp), 512, tr["kt"][kp], rROPE, rRP1, rRP2)
                yield
                for blk in range(4):
                    for (srcf, sreg, dstf, dreg) in ((kt, tr["kt"][kp], ktok, tr["ktok"][kp]),
                                                     (vt, tr["vt"][kp], vtok, tr["vtok"][kp])):
                        pb = blk % 2
                        pt = self.PTA if pb == 0 else self.PTB
                        self.E("tensor", lambda t, p=pt, srcf=srcf, kp=kp, blk=blk: t.transpose(
                            p[:, 0:128], srcf(kp)[:, blk * 128:(blk + 1) * 128], d.IDNB[:]),
                            reads=[sreg, d.rCONST], writes=[self.rPT[pb]])
                        self.E("vector", lambda v, p=pt, dstf=dstf, kp=kp, blk=blk: v.tensor_copy(
                            dstf(kp)[:, blk * 128:(blk + 1) * 128], p[:, 0:128]),
                            reads=[self.rPT[pb]], writes=[dreg])
                    yield
            wb = hp
            ws = d.WS[:, wb * 4096:(wb + 1) * 4096]
            self.DMA("sync", ws, self.WqB[h], f"ws{wb}",
                     reads=[self.rWB["Wq"]], writes=[d.rWS[wb]])
            def emitq(t, ws=ws, xm=xm):
                last_i = None
                for c in range(CF):
                    last_i = t.matmul(self.pk(1), ws[:, c * 128:(c + 1) * 128],
                                      xm(c, 0, 512), start=(c == 0), stop=(c == CF - 1))
                return last_i
            self.E("tensor", emitq, reads=[d.rWS[wb], rXMB], writes=[self.rPK[1]])
            self.E("vector", lambda v, hp=hp: v.tensor_copy(qts(hp), self.pk(1)),
                   reads=[self.rPK[1]], writes=[tr["qt"][hp]])
            yield
            self._rope(qts(hp), 512, tr["qt"][hp], rROPE, rRP1, rRP2)
            yield

        def gen_B(bi, h):
            kvh = h // 4
            kp = kvh % 2
            hp = h % 2
            self.DMA("sync", d.GROWH[0:1, 0:512], d.GTB[h:h + 1, 0:512],
                     "grw", reads=[rGTB], writes=[rGRW])
            st_ap = d.STATES[:, h * 128:(h + 1) * 128]
            for blk in range(4):
                self.E("vector", lambda v, st=st_ap: v.tensor_copy(d.STB[:], st),
                       reads=[rST], writes=[rSTB])
                self.E("tensor", lambda t, kp=kp, hp=hp, blk=blk: t.matmul(
                    self.pk(2)[:, 0:128], kt(kp)[:, blk * 128:(blk + 1) * 128],
                    qts(hp)[:, blk * 128:(blk + 1) * 128], start=True, stop=True),
                    reads=[tr["kt"][kp], tr["qt"][hp]], writes=[self.rPK[2]])
                self.E("tensor", lambda t, blk=blk: t.matmul(
                    self.pk(3)[:, 0:128], d.ONE1F[:],
                    d.GROWH[0:1, blk * 128:(blk + 1) * 128], start=True, stop=True),
                    reads=[rGRW, d.rCONST], writes=[self.rPK[3]])
                gcol = d.GQ[:, blk * 32 + h:blk * 32 + h + 1] if False else \
                    d.GQ[:, blk * 32 + h:blk * 32 + h + 1]
                gcol = d.GQ[:, blk * 32 + h:blk * 32 + h + 1]
                self.E("vector", lambda v, g=gcol: v.scalar_tensor_tensor(
                    d.DECT[:], self.pk(3)[:, 0:128], g, d.BIAST[:],
                    ALU.subtract, ALU.add),
                    reads=[self.rPK[3], rG, d.rCONST], writes=[rDEC])
                self.E("scalar", lambda s: s.activation(d.DECT[:], d.DECT[:], AF.Exp),
                       reads=[rDEC], writes=[rDEC])
                self.E("vector", lambda v: v.tensor_tensor(d.SDT[:], self.pk(2)[:, 0:128],
                                                           d.DECT[:], ALU.mult),
                       reads=[self.rPK[2], rDEC], writes=[rSD])
                edcol = d.EDQ[:, blk * 32 + h:blk * 32 + h + 1]
                dtcol = d.DTQ[:, blk * 32 + h:blk * 32 + h + 1]
                self.E("vector", lambda v, e=edcol, kp=kp, blk=blk: v.tensor_scalar(
                    d.KDT[:], ktok(kp)[:, blk * 128:(blk + 1) * 128], e, None, ALU.mult),
                    reads=[tr["ktok"][kp], rED], writes=[rKD])
                self.E("vector", lambda v, e=dtcol, kp=kp, blk=blk: v.tensor_scalar(
                    d.XCT[:], vtok(kp)[:, blk * 128:(blk + 1) * 128], e, None, ALU.mult),
                    reads=[tr["vtok"][kp], rDT], writes=[rXC])
                def emity(t, hp=hp, blk=blk):
                    t.matmul(self.pk(4)[:, 0:128], d.SDT[:], d.XCT[:],
                             start=True, stop=True)
                    return t.matmul(self.pk(4)[:, 128:256],
                                    qts(hp)[:, blk * 128:(blk + 1) * 128], d.STB[:],
                                    start=True, stop=True)
                self.E("tensor", emity, reads=[rSD, rXC, tr["qt"][hp], rSTB],
                       writes=[self.rPK[4]])
                self.E("tensor", lambda t: t.matmul(self.pk(5)[:, 0:128], d.KDT[:],
                                                    d.XCT[:], start=True, stop=True),
                       reads=[rKD, rXC], writes=[self.rPK[5]])
                egcol = d.EGQ[:, blk * 32 + h:blk * 32 + h + 1]
                self.E("vector", lambda v, e=egcol: v.tensor_scalar(
                    d.YSB[:], self.pk(4)[:, 128:256], e, None, ALU.mult),
                    reads=[self.rPK[4], rEG], writes=[rYS])
                self.E("vector", lambda v: v.tensor_tensor(d.YSB[:], d.YSB[:],
                                                           self.pk(4)[:, 0:128], ALU.add),
                       reads=[self.rPK[4], rYS], writes=[rYS])
                sl8 = h % 8
                self.E("scalar", lambda s, sl8=sl8, blk=blk: s.activation(
                    d.TRS2[:], d.YSB[:], AF.Square,
                    accum_out=d.SSH[:, sl8 * 4 + blk:sl8 * 4 + blk + 1]),
                    reads=[rYS], writes=[rSSH[sl8]])
                self.E("vector", lambda v: v.tensor_copy(d.YB16[:], d.YSB[:]),
                       reads=[rYS], writes=[rYB])
                pb = blk % 2
                pt = self.PTA if pb == 0 else self.PTB
                self.E("tensor", lambda t, p=pt: t.transpose(p[:, 0:128], d.YB16[:],
                                                             d.IDNB[:]),
                       reads=[rYB, d.rCONST], writes=[self.rPT[pb]])
                self.E("vector", lambda v, p=pt, sl8=(h % 8), blk=blk: v.tensor_copy(
                    yth(sl8)[:, blk * 128:(blk + 1) * 128], p[:, 0:128]),
                    reads=[self.rPT[pb]], writes=[tr["yth"][h % 8]])
                cdcol = d.CDQ[:, blk * 32 + h:blk * 32 + h + 1]
                self.E("vector", lambda v, c=cdcol, st=st_ap: v.scalar_tensor_tensor(
                    st, st, c, self.pk(5)[:, 0:128], ALU.mult, ALU.add),
                    reads=[rST, rCD, self.rPK[5]], writes=[rST])
                yield

        def gen_C4(bi, h0, t0, xm):
            for hh in range(h0, h0 + 4):
                sl8 = hh % 8
                self.E("scalar", lambda s, sl8=sl8: s.activation(
                    d.RSTDH[:, sl8 * 4:sl8 * 4 + 4], d.SSH[:, sl8 * 4:sl8 * 4 + 4], AF.Ln,
                    scale=1.0 / 128.0, bias=d.EPSB[:]),
                    reads=[rSSH[sl8], d.rCONST], writes=[rRSH[sl8]])
                self.E("scalar", lambda s, sl8=sl8: s.activation(
                    d.RSTDH[:, sl8 * 4:sl8 * 4 + 4], d.RSTDH[:, sl8 * 4:sl8 * 4 + 4], AF.Exp,
                    scale=-0.5),
                    reads=[rRSH[sl8]], writes=[rRSH[sl8]])
                yield
            for hh in range(h0, h0 + 4):
                sl8 = hh % 8
                self.E("tensor", lambda t, sl8=sl8: t.transpose(
                    self.pk(2, (0, 4))[:, 0:128], d.RSTDH[:, sl8 * 4:sl8 * 4 + 4], d.IDN[:]),
                    reads=[rRSH[sl8], d.rCONST], writes=[self.rPK[2]])
                self.E("vector", lambda v: v.tensor_copy(d.RSTT[0:4, :],
                                                         self.pk(2, (0, 4))[:, 0:128]),
                       reads=[self.rPK[2]], writes=[rRST])
                self.DMA("gpsimd", d.RSROW[0:1, 0:512], d.RSTT[0:4, :], "rsw",
                         reads=[rRST], writes=[rRSR])
                wb = hh % 2
                ws = d.WS[:, wb * 4096:(wb + 1) * 4096]
                self.DMA("sync", ws, self.WgB[hh], f"ws{wb}",
                         reads=[self.rWB["Wg"]], writes=[d.rWS[wb]])
                def emitg(t, ws=ws, xm=xm):
                    last_i = None
                    for c in range(CF):
                        last_i = t.matmul(self.pk(0), ws[:, c * 128:(c + 1) * 128],
                                          xm(c, 0, 512), start=(c == 0), stop=(c == CF - 1))
                    return last_i
                self.E("tensor", emitg, reads=[d.rWS[wb], rXMB], writes=[self.rPK[0]])
                self.E("scalar", lambda s: s.activation(d.TRS[:, 0:512], self.pk(0), AF.Silu),
                       reads=[self.rPK[0]], writes=[rRP1])
                self.E("vector", lambda v: v.tensor_scalar(
                    d.TRS[:, 0:512], d.TRS[:, 0:512], d.GN[:], None, ALU.mult),
                    reads=[rRP1, d.rCONST], writes=[rRP1])
                self.E("tensor", lambda t: t.matmul(
                    self.pk(3), d.ONE1F[:], d.RSROW[0:1, 0:512], start=True, stop=True),
                    reads=[rRSR, d.rCONST], writes=[self.rPK[3]])
                self.E("vector", lambda v, sl8=sl8: v.tensor_tensor(
                    d.RP2[:, 0:512], yth(sl8)[:, 0:512], self.pk(3), ALU.mult),
                    reads=[tr["yth"][sl8], self.rPK[3]], writes=[rRP2])
                self.E("vector", lambda v: v.tensor_tensor(
                    d.RP2[:, 0:512], d.RP2[:, 0:512], d.TRS[:, 0:512], ALU.mult),
                    reads=[rRP1, rRP2], writes=[rRP2])
                self.DMA("scalar", self.yTb[hh, :, t0:t0 + 512], d.RP2[:, 0:512],
                         f"yb{hh % 2}", reads=[rRP2], writes=[self.rYTb])
                yield

        blocks = [(0, 512), (512, 512), (1024, 512), (1536, 512)]
        for bi, (t0, Tb) in enumerate(blocks):
            rXMB.turnover()
            yield from self.ph_xm(2, self.xT2, t0, Tb, 0, rXMB)
            xm = lambda c, a, w, Tb=Tb: d.BIG[:, c * Tb + a: c * Tb + a + w]
            self.DMA("sync", d.ROPEC[:, 0:Tb], d.cosT_d[:, t0:t0 + Tb], "rpc", writes=[rROPE])
            self.DMA("sync", d.ROPES[:, 0:Tb], d.sinTs_d[:, t0:t0 + Tb], "rps", writes=[rROPE])
            for blk in range(4):
                def emit(t, blk=blk, xm=xm):
                    last_i = None
                    for c in range(CF):
                        last_i = t.matmul(self.pk(4)[:, 0:32], xm(c, blk * 128, 128),
                                          d.WINTB[:, c * 32:(c + 1) * 32],
                                          start=(c == 0), stop=(c == CF - 1))
                    return last_i
                self.E("tensor", emit, reads=[rXMB, d.rCONST], writes=[self.rPK[4]])
                dtr = d.DTQ[:, blk * 32:(blk + 1) * 32]
                self.E("vector", lambda v, o=dtr: v.tensor_tensor(o, self.pk(4)[:, 0:32],
                                                                  d.DTBB[:, 0:32], ALU.add),
                       reads=[self.rPK[4], d.rCONST], writes=[rDT])
                a1 = d.YSB[:, 0:32]; a2 = d.YSB[:, 32:64]; a3 = d.YSB[:, 64:96]
                self.E("vector", lambda v, i=dtr: v.scalar_tensor_tensor(
                    a1, i, -1.0, i, ALU.mult, ALU.max), reads=[rDT], writes=[rYS])
                self.E("scalar", lambda s: s.activation(a2, a1, AF.Exp, scale=-1.0),
                       reads=[rYS], writes=[rYS])
                self.E("scalar", lambda s: s.activation(a2, a2, AF.Ln, bias=1.0),
                       reads=[rYS], writes=[rYS])
                self.E("vector", lambda v, i=dtr: v.tensor_scalar_max(a3, i, 0.0),
                       reads=[rDT], writes=[rYS])
                self.E("vector", lambda v, o=dtr: v.tensor_add(o, a2, a3),
                       reads=[rYS], writes=[rDT])
                gg = d.GGQ[:, blk * 32:(blk + 1) * 32]
                self.E("vector", lambda v, o=gg, i=dtr: v.tensor_tensor(o, i, d.ABB[:, 0:32],
                                                                        ALU.mult),
                       reads=[rDT, d.rCONST], writes=[rGG])
                self.E("tensor", lambda t, i=gg: t.matmul(self.pk(5)[:, 0:32], d.UTRI[:], i,
                                                          start=True, stop=True),
                       reads=[rGG, d.rCONST], writes=[self.rPK[5]])
                gq = d.GQ[:, blk * 32:(blk + 1) * 32]
                self.E("vector", lambda v, o=gq: v.tensor_copy(o, self.pk(5)[:, 0:32]),
                       reads=[self.rPK[5]], writes=[rG])
                eg = d.EGQ[:, blk * 32:(blk + 1) * 32]
                self.E("scalar", lambda s, o=eg, i=gq: s.activation(o, i, AF.Exp),
                       reads=[rG], writes=[rEG])
                self.E("tensor", lambda t, i=gq: t.transpose(
                    self.pk(2, (0, 32))[:, 0:128], i, d.IDN[:]),
                    reads=[rG, d.rCONST], writes=[self.rPK[2]])
                self.E("vector", lambda v, blk=blk: v.tensor_copy(
                    d.GTB[0:32, blk * 128:(blk + 1) * 128], self.pk(2, (0, 32))[:, 0:128]),
                    reads=[self.rPK[2]], writes=[rGTB])
                self.E("tensor", lambda t, blk=blk: t.transpose(
                    self.pk(3, (0, 1))[:, 0:32],
                    d.GTB[0:32, blk * 128 + 127:blk * 128 + 128], d.IDN[0:32, 0:32]),
                    reads=[rGTB, d.rCONST], writes=[self.rPK[3]])
                self.E("vector", lambda v: v.tensor_copy(d.SROW[0:1, 1536:1568],
                                                         self.pk(3, (0, 1))[:, 0:32]),
                       reads=[self.rPK[3]], writes=[d.rSROW[3]])
                self.E("tensor", lambda t: t.matmul(self.pk(3)[:, 0:32], d.ONE1F[:],
                                                    d.SROW[0:1, 1536:1568],
                                                    start=True, stop=True),
                       reads=[d.rSROW[3], d.rCONST], writes=[self.rPK[3]])
                self.E("vector", lambda v: v.tensor_copy(d.GLB[:], self.pk(3)[:, 0:32]),
                       reads=[self.rPK[3]], writes=[rGLB])
                ed = d.EDQ[:, blk * 32:(blk + 1) * 32]
                self.E("vector", lambda v, o=ed, i=gq: v.tensor_tensor(o, d.GLB[:], i,
                                                                       ALU.subtract),
                       reads=[rGLB, rG], writes=[rED])
                self.E("scalar", lambda s, o=ed: s.activation(o, o, AF.Exp),
                       reads=[rED], writes=[rED])
                cd = d.CDQ[:, blk * 32:(blk + 1) * 32]
                self.E("scalar", lambda s, o=cd: s.activation(o, d.GLB[:], AF.Exp),
                       reads=[rGLB], writes=[rCD])
                yield
            for _ in gen_A(bi, 0, xm):
                yield
            pendC = iter(())
            for h in range(32):
                ga = gen_A(bi, h + 1, xm) if h + 1 < 32 else iter(())
                gb = gen_B(bi, h)
                b_alive = a_alive = True
                while b_alive or a_alive:
                    if b_alive:
                        try:
                            next(gb)
                        except StopIteration:
                            b_alive = False
                    if a_alive:
                        try:
                            next(ga)
                        except StopIteration:
                            a_alive = False
                    try:
                        next(pendC)
                    except StopIteration:
                        pass
                    yield
                if h % 4 == 3:
                    import itertools as _it
                    pendC = _it.chain(pendC, gen_C4(bi, h - 3, t0, xm))
            for _ in pendC:
                yield
            rSLAB = self.bignew(1)[0]
            self.big_live = [rSLAB, rXMB] + [rg for nm in tr for rg in tr[nm]]
            for c in range(CF):
                self.DMA("sync", d.BIG[:, c * Tb:(c + 1) * Tb], self.yTb[c, :, t0:t0 + Tb],
                         f"ybl{bi}", reads=[self.rYTb], writes=[rSLAB])
            for mb in range(32):
                wb = mb % 2
                ws = d.WS[:, wb * 4096:(wb + 1) * 4096]
                self.DMA("sync", ws, self.WoB[mb], f"ws{wb}",
                         reads=[self.rWB["Wo"]], writes=[d.rWS[wb]])
                pko = 1 if mb % 2 == 0 else 3
                def emito(t, ws=ws, Tb=Tb, pko=pko):
                    last_i = None
                    for c in range(CF):
                        last_i = t.matmul(self.pk(pko), ws[:, c * 128:(c + 1) * 128],
                                          d.BIG[:, c * Tb: c * Tb + 512],
                                          start=(c == 0), stop=(c == CF - 1))
                    return last_i
                self.E("tensor", emito, reads=[d.rWS[wb], rSLAB], writes=[self.rPK[pko]])
                rb = mb % 2
                res = d.RES[:, rb * 512:(rb + 1) * 512]
                self.DMA("sync", res, self.xT2[mb * 128:(mb + 1) * 128, t0:t0 + 512],
                         f"xr{rb}", reads=[self.xreg(2, t0 // 512)], writes=[d.rRES[rb]])
                xb = mb % 2
                xn = d.XNEW[:, xb * 512:(xb + 1) * 512]
                self.E("vector", lambda v, o=xn, r=res, pko=pko: v.tensor_tensor(
                    o, self.pk(pko), r, ALU.add),
                    reads=[self.rPK[pko], d.rRES[rb]], writes=[d.rXNEW[xb]])
                self.DMA("scalar", self.xT3[mb * 128:(mb + 1) * 128, t0:t0 + 512],
                         xn, f"xw{xb}", reads=[d.rXNEW[xb]],
                         writes=[self.xreg(3, t0 // 512)])
                eb = mb % 2
                xbf = d.EVB[:, eb * 512:(eb + 1) * 512]
                sqb = d.EVB[:, 1024 + eb * 512:1024 + (eb + 1) * 512]
                self.E("vector", lambda v, a=xbf, i=xn: v.tensor_copy(a, i),
                       reads=[d.rXNEW[xb]], writes=[d.rEVB[eb]])
                self.E("scalar", lambda sc, a=sqb, i=xbf: sc.activation(a, i, AF.Square),
                       reads=[d.rEVB[eb]], writes=[d.rEVB[2 + eb]])
                self.E("tensor", lambda t, a=xbf, st=(mb == 0), z=(mb == 31): t.matmul(
                    self.pk(2, (0, 1)), d.ONECB[:], a, start=st, stop=z),
                    reads=[d.rCONST, d.rEVB[eb]], writes=[self.rPK[2]])
                self.E("tensor", lambda t, a=sqb, st=(mb == 0), z=(mb == 31): t.matmul(
                    self.pk(4, (0, 1)), d.ONECB[:], a, start=st, stop=z),
                    reads=[d.rCONST, d.rEVB[2 + eb]], writes=[self.rPK[4]])
                yield
            if bi == 0:
                self.rMUB.turnover()
                self.rRSB.turnover()
            self._stats_rows(t0, 512, 2, 4)
            yield


    def _rope(self, buf, Tb, rbuf, rROPE, rRP1, rRP2):
        d = self
        self.E("gpsimd", lambda g: g.tensor_copy(d.RP2[0:64, 0:Tb], buf[64:128, 0:Tb]),
               reads=[rbuf], writes=[rRP2])
        self.E("gpsimd", lambda g: g.tensor_copy(d.RP2[64:128, 0:Tb], buf[0:64, 0:Tb]),
               reads=[rbuf], writes=[rRP2])
        self.E("gpsimd", lambda g: g.tensor_mul(d.TRS[:, 0:Tb], buf[:, 0:Tb],
                                                d.ROPEC[:, 0:Tb]),
               reads=[rbuf, rROPE], writes=[rRP1])
        self.E("gpsimd", lambda g: g.tensor_mul(d.RP2[:, 0:Tb], d.RP2[:, 0:Tb],
                                                d.ROPES[:, 0:Tb]),
               reads=[rRP2, rROPE], writes=[rRP2])
        self.E("gpsimd", lambda g: g.tensor_add(buf[:, 0:Tb], d.TRS[:, 0:Tb],
                                                d.RP2[:, 0:Tb]),
               reads=[rRP1, rRP2], writes=[rbuf])


def build(stages=4, dbg_hook=None):
    import itertools
    b = B(stages)
    b.ph_consts()
    side_parts = [b.ph_cast(b.WqkvL, b.WqkvB, 24, DM, 0, b.rWB["Wqkv"]),
                  b.ph_cast(b.WoaL, b.WoaB, 32, 1024, 1, b.rWB["Woa"])]
    if stages >= 3:
        side_parts += [b.ph_cast(b.WqL, b.WqB, 32, DM, 2, b.rWB["Wq"]),
                       b.ph_cast(b.WkL, b.WkB, 8, DM, 3, b.rWB["Wk"]),
                       b.ph_cast(b.WvL, b.WvB, 8, DM, 4, b.rWB["Wv"])]
    if stages >= 2:
        side_parts.append(b.ph_prep())
    if stages >= 3:
        side_parts += [b.ph_cast(b.WgL, b.WgB, 32, DM, 5, b.rWB["Wg"]),
                       b.ph_cast(b.WoL, b.WoB, 32, DM, 6, b.rWB["Wo"])]
    side = itertools.chain(*side_parts)
    main_parts = [b.ph_attn_a(), b.ph_attn_a_out()]
    if stages >= 2:
        main_parts.append(b.ph_ffn(b.xT1, 1, b.xT2, 2, last=(stages == 2)))
    if stages >= 3:
        main_parts.append(b.ph_attn_b())
    if stages >= 4:
        main_parts.append(b.ph_ffn(b.xT3, 3, b.yT, 4, last=True))
    main = itertools.chain(*main_parts)
    while True:
        try:
            next(main)
        except StopIteration:
            break
        for _ in range(3):
            try:
                next(side)
            except StopIteration:
                break
    for _ in side:
        pass
    if dbg_hook is not None:
        dbg_hook(b)
    b.finish()
    b.w.emit()
    return b

def kernel(**inputs):
    hp = host_prepare(inputs)
    x = np.asarray(inputs["L0_attn_a_x"], dtype=np.float32)
    b = build(4)
    in_maps = []
    for core in range(8):
        im = dict(hp)
        im["xT0"] = np.ascontiguousarray(x[core].T)
        in_maps.append(im)
    res = run_bass_kernel_spmd(b.nc, in_maps, core_ids=list(range(8)))
    out = np.stack([np.asarray(r["yT"], dtype=np.float32).T for r in res.results])
    return np.ascontiguousarray(out)
